# version 14
# baseline (speedup 1.0000x reference)
import numpy as np
import concourse.bass as bass
import concourse.mybir as mybir
from concourse.bass_utils import run_bass_kernel_spmd

F32 = mybir.dt.float32
BF16 = mybir.dt.bfloat16
AF = mybir.ActivationFunctionType
ALU = mybir.AluOpType

T = 4096
D = 1024
DC = 8
TG = 512
NG = T // TG
NBANK = 8192
NCH = NBANK // 128
NW = 16
I_SEL = 4096
IT = 32
E = 4
SQRT_N = float(np.sqrt(np.float32(NBANK)))

MU, RR, E2, N2, TMP = 0, 512, 1024, 1536, 2048
LCAT, MASK = 2560, 4608
ROWLEN = 6656
MODR = 0


def build_nc(ng=NG, dbg=False):
    nc = bass.Bass("TRN2", target_bir_lowering=False)

    x_d = nc.dram_tensor("x", [T, D], F32, kind="ExternalInput").ap()
    c_d = nc.dram_tensor("c", [1, D], F32, kind="ExternalInput").ap()
    k_d = nc.dram_tensor("k", [NBANK, D], F32, kind="ExternalInput").ap()
    v_d = nc.dram_tensor("v", [NBANK, D], F32, kind="ExternalInput").ap()
    wr_d = nc.dram_tensor("wr", [D, E], F32, kind="ExternalInput").ap()
    w1_d = nc.dram_tensor("w1", [E, D, I_SEL], F32, kind="ExternalInput").ap()
    b1_d = nc.dram_tensor("b1", [E, I_SEL], F32, kind="ExternalInput").ap()
    wo_d = nc.dram_tensor("wo", [I_SEL, D], F32, kind="ExternalInput").ap()
    bo_d = nc.dram_tensor("bo", [D], F32, kind="ExternalInput").ap()
    gam_d = nc.dram_tensor("gam", [D], F32, kind="ExternalInput").ap()
    bet_d = nc.dram_tensor("bet", [D], F32, kind="ExternalInput").ap()
    wmod_d = nc.dram_tensor("wmod", [D, 6 * D], F32, kind="ExternalInput").ap()
    out_d = nc.dram_tensor("out", [T, D], F32, kind="ExternalOutput").ap()

    vbf = nc.dram_tensor("vbf", [NBANK, D], BF16).ap()
    w1bf = nc.dram_tensor("w1bf", [E, D, I_SEL], BF16).ap()
    wobf = nc.dram_tensor("wobf", [I_SEL, D], BF16).ap()
    ktd = nc.dram_tensor("ktd", [NW, 128, DC * 512], BF16).ap()
    gdr = nc.dram_tensor("gdr", [D, D], BF16).ap()
    mkind = dict(kind="ExternalOutput") if dbg else {}
    msc1 = nc.dram_tensor("msc1", [E * TG], F32, **mkind).ap()
    msc2 = nc.dram_tensor("msc2", [E * TG], F32, **mkind).ap()
    if dbg:
        dbg_t = {}
        for nm, shp, dt in [("dbg_h", [128, 4096], F32), ("dbg_x2", [128, 4096], F32),
                            ("dbg_h2", [128, 4096], F32), ("dbg_gated", [128, 32768], F32),
                            ("dbg_y2", [128, 4096], F32), ("dbg_gsel", [128, 16384], F32),
                            ("dbg_h2m", [128, 16384], F32), ("dbg_rowb", [1, ROWLEN], F32),
                            ("dbg_pp", [128, DC, 16], F32), ("dbg_c", [128, DC], F32),
                            ("dbg_gdr", [D, D], F32), ("dbg_ktd", [NW, 128, DC * 512], F32),
                            ("dbg_P", [128, 4096], F32), ("dbg_hh", [128, 4096], F32)]:
            dbg_t[nm] = nc.dram_tensor(nm, shp, dt, kind="ExternalOutput").ap()

    from contextlib import ExitStack
    with ExitStack() as st:
        en = st.enter_context
        sb_pp = en(nc.sbuf_tensor("sb_pp", [128, DC, 16], F32))
        sb_c = en(nc.sbuf_tensor("sb_c", [128, DC], F32))
        sb_wr = en(nc.sbuf_tensor("sb_wr", [128, DC, E], F32))
        sb_b1 = en(nc.sbuf_tensor("sb_b1", [E, I_SEL], BF16))
        sb_m4 = en(nc.sbuf_tensor("sb_m4", [E, TG], F32))
        sb_m4b = en(nc.sbuf_tensor("sb_m4b", [E, TG], BF16))
        sb_l4 = en(nc.sbuf_tensor("sb_l4", [E, TG], F32))
        rowb = en(nc.sbuf_tensor("rowb", [1, ROWLEN], F32))
        idf = en(nc.sbuf_tensor("idf", [128, 128], F32))
        idb = en(nc.sbuf_tensor("idb", [128, 128], BF16))
        iot = en(nc.sbuf_tensor("iot", [128, 128], F32))
        onf = en(nc.sbuf_tensor("onf", [128, 1], F32))
        onb = en(nc.sbuf_tensor("onb", [128, 1], BF16))
        on1b = en(nc.sbuf_tensor("on1b", [128, 1], BF16))
        o1r = en(nc.sbuf_tensor("o1r", [1, 128], F32))
        xtm = en(nc.sbuf_tensor("xtm", [128, 2, 1024], F32))
        xg = en(nc.sbuf_tensor("xg", [128, DC * TG], F32))
        hbuf = en(nc.sbuf_tensor("hbuf", [128, DC * TG], BF16))
        xsq = en(nc.sbuf_tensor("xsq", [128, DC * TG], BF16))
        reps = en(nc.sbuf_tensor("reps", [128, 2, TG], F32))
        scr = en(nc.sbuf_tensor("scr", [128, TG], F32))
        big = en(nc.sbuf_tensor("big", [128, 65536 // 2], BF16))
        winp = en(nc.sbuf_tensor("winp", [128, 8192], BF16))
        vwin = en(nc.sbuf_tensor("vwin", [128, 8192], BF16))
        h2p = en(nc.sbuf_tensor("h2p", [128, DC * TG], F32))
        y2b = en(nc.sbuf_tensor("y2b", [128, DC * TG], BF16))

        ps1 = en(nc.psum_tensor("ps1", [128, 512], F32))
        ps2 = en(nc.psum_tensor("ps2", [128, 512], F32))
        ps5 = en(nc.psum_tensor("ps5", [128, 512], F32))
        ps6 = en(nc.psum_tensor("ps6", [128, 512], F32))
        ps4 = en(nc.psum_tensor("ps4", [128, 2048], F32))

        sem = {s: en(nc.semaphore(s)) for s in ["SD", "GD", "TT", "VV", "SS", "SC"]}

        ops = {e: [] for e in ["T", "V", "S", "Y", "G"]}
        cnt = {s: 0 for s in sem}
        mark = {}

        def emit(eng, fn, waits=(), inc=None, n=1, tag=None):
            w = [(s, v) for (s, v) in waits if v > 0]
            ops[eng].append((w, fn, inc, n))
            if inc is not None:
                cnt[inc] += n
            if tag is not None:
                mark[tag] = cnt[inc]

        def at(tag):
            return mark[tag]

        def dma(eng, sdsem, out, in_, waits=(), tag=None, slow=False):
            def fn(EE):
                if slow:
                    return EE.dma_start(out, in_, allow_slow_non_contiguous=True)
                return EE.dma_start(out, in_)
            emit(eng, fn, waits, inc=sdsem, n=16, tag=tag)

        emit("G", lambda g: g.iota(iot[:], [[-1, 128]], channel_multiplier=1,
                                   allow_small_or_imprecise_dtypes=True),
             inc="GD", n=1, tag="iota")
        emit("V", lambda v: v.tensor_single_scalar(idf[:], iot[:], 0.0, ALU.is_equal),
             waits=[("GD", at("iota"))], inc="VV", tag="idf")
        emit("V", lambda v: v.tensor_copy(idb[:], idf[:]), inc="VV", tag="idb")
        emit("V", lambda v: v.memset(onf[:], 1.0 / 1024.0), inc="VV")
        emit("V", lambda v: v.memset(onb[:], 1.0 / 1024.0), inc="VV")
        emit("V", lambda v: v.memset(on1b[:], 1.0), inc="VV")
        emit("V", lambda v: v.memset(o1r[:], 1.0), inc="VV", tag="ones")

        dma("G", "GD", sb_c[:].rearrange("p c -> p c"),
            c_d.rearrange("one (c p) -> p (one c)", p=128), tag="c_ld", slow=True)
        dma("G", "GD", sb_wr[:].rearrange("p c e -> p c e"),
            wr_d.rearrange("(c p) e -> p c e", p=128), tag="wr_ld", slow=True)
        dma("G", "GD", sb_b1[:], b1_d, tag="b1_ld")


        emit("S", lambda s: s.activation(sb_c[:], sb_c[:], AF.Silu),
             waits=[("GD", at("c_ld"))], inc="SS", tag="siluc")

        modslc = [ps4[0:1, 0:512], ps4[0:1, 512:1024], ps4[0:1, 1024:1536],
                  ps4[0:1, 1536:2048], ps1[0:1, :], ps2[0:1, :]]
        for jh in range(2):
            for dc in range(DC):
                wv = (xg if dc % 2 == 0 else h2p)[:, 0:3072]
                waits = [("SS", at("siluc"))]
                seq = jh * DC + dc
                if seq >= 2:
                    pjh, pdc = divmod(seq - 2, DC)
                    waits.append(("TT", at(f"modmm{pjh}_{pdc}")))
                dma("Y", "SD", wv,
                    wmod_d[dc * 128:(dc + 1) * 128, jh * 3072:(jh + 1) * 3072],
                    waits=waits, tag=f"modw{jh}_{dc}")

                def mk(jh=jh, dc=dc):
                    def fn(t):
                        r = None
                        src = (xg if dc % 2 == 0 else h2p)
                        for jw in range(6):
                            wv2 = src[:, jw * 512:(jw + 1) * 512]
                            r = t.matmul(modslc[jw], sb_c[:, dc:dc + 1], wv2,
                                         start=(dc == 0), stop=(dc == DC - 1))
                        return r
                    return fn
                mmw = [("SD", at(f"modw{jh}_{dc}"))]
                if jh == 1 and dc == 0:
                    mmw.append(("VV", at("modev0")))
                emit("T", mk(), waits=mmw, inc="TT", tag=f"modmm{jh}_{dc}")
            def mkev(jh=jh):
                def fn(v):
                    r = None
                    for jw in range(6):
                        r = v.tensor_copy(rowb[0:1, jh * 3072 + jw * 512: jh * 3072 + (jw + 1) * 512],
                                          modslc[jw])
                    return r
                return fn
            emit("V", mkev(), waits=[("TT", at(f"modmm{jh}_{DC-1}"))], inc="VV",
                 tag=f"modev{jh}")

        dma("G", "GD", xg[0:1, 2048:3072],
            gam_d.rearrange("(one d) -> one d", one=1),
            waits=[("TT", at("modmm1_6"))], tag="gam_ld")
        dma("G", "GD", xg[0:1, 3072:4096],
            bet_d.rearrange("(one d) -> one d", one=1),
            waits=[("TT", at("modmm1_6"))], tag="bet_ld")

        def R(slot, off=0, ln=1024):
            return rowb[0:1, slot * 1024 + off: slot * 1024 + off + ln]

        def mkderive2():
            def fn(v):
                v.tensor_scalar_add(R(1), R(1), 1.0)
                v.tensor_scalar_add(R(4), R(4), 1.0)
                for sl in (2, 5):
                    tmp = xg[0:1, 0:1024]
                    tmp2 = xg[0:1, 1024:2048]
                    v.tensor_single_scalar(tmp, R(sl), 0.0, ALU.is_ge)
                    v.tensor_scalar(tmp, tmp, 2.0, -1.0, ALU.mult, ALU.add)
                    v.tensor_scalar_mul(tmp2, R(sl), -1.0)
                    v.tensor_tensor(tmp2, R(sl), tmp2, ALU.max)
                    v.tensor_scalar_max(tmp2, tmp2, 0.25)
                    v.tensor_tensor(R(sl), tmp, tmp2, ALU.mult)
                gmr = xg[0:1, 2048:3072]
                btr = xg[0:1, 3072:4096]
                v.tensor_tensor(gmr, gmr, R(5), ALU.mult)
                v.tensor_tensor(btr, btr, R(5), ALU.mult)
                return v.engine_nop()
            return fn
        emit("V", mkderive2(),
             waits=[("VV", at("modev1")), ("GD", at("bet_ld"))], inc="VV", tag="derive")

        dma("G", "GD", xtm[0:5, 0, :], rowb[0:1, 0:5 * 1024],
            waits=[("VV", at("derive"))], tag="ppw0")
        dma("G", "GD", xtm[5:7, 0, :], xg[0:1, 2048:4096],
            waits=[("VV", at("derive"))], tag="ppw1")
        dma("G", "GD", xtm[7:8, 0, :],
            bo_d.rearrange("(one d) -> one d", one=1), tag="ppw2")
        for dc in range(DC):
            def mkppt(dc=dc):
                def fn(t):
                    return t.transpose(ps6[:, 0:8],
                                       xtm[0:8, 0, dc * 128:(dc + 1) * 128], idf[0:8, 0:8])
                return fn
            wl = [("GD", at("ppw2")), ("TT", at("modmm1_7"))]
            if dc > 0:
                wl.append(("VV", at(f"ppv{dc-1}")))
            emit("T", mkppt(dc), waits=wl, inc="TT", tag=f"ppt{dc}")
            emit("V", lambda v, dc=dc: v.tensor_copy(sb_pp[:, dc, 0:8], ps6[:, 0:8]),
                 waits=[("TT", at(f"ppt{dc}"))], inc="VV", tag=f"ppv{dc}")
        emit("V", lambda v: v.engine_nop(), waits=[("VV", at(f"ppv{DC-1}"))],
             inc="VV", tag="ppdone")
        def emit_staging_casts():
            for j in range(4):
                dma("G", "GD", vbf[j * 2048:(j + 1) * 2048, :], v_d[j * 2048:(j + 1) * 2048, :],
                    tag=f"vbf{j}")
            for e in range(E):
                for h in range(2):
                    dma("G", "GD", w1bf[e, h * 512:(h + 1) * 512, :],
                        w1_d[e, h * 512:(h + 1) * 512, :], tag=f"w1bf{e}_{h}")
            for h in range(2):
                dma("G", "GD", wobf[h * 2048:(h + 1) * 2048, :], wo_d[h * 2048:(h + 1) * 2048, :],
                    tag=f"wobf{h}")
        PP_SA, PP_CA1, PP_GA, PP_SB, PP_CB1, PP_GGB, PP_BGB, PP_BO = 0, 1, 2, 3, 4, 5, 6, 7

        tposes = 0
        for w in range(NW):
            kcb = (xg if w % 2 == 0 else h2p)[:, 0:4096]
            waits = [("TT", at("modmm1_7")), ("GD", at("ppw1"))]
            if w < 2:
                waits.append(("VV", at("derive")))
            if w >= 2:
                waits.append(("TT", at(f"ktt{w-2}")))
            dma("Y", "SD", kcb.rearrange("p (a d) -> p a d", d=D),
                k_d[w * 512:(w + 1) * 512, :].rearrange("(a p) d -> p a d", p=128),
                waits=waits, tag=f"kc{w}")
            stg = winp[:, (w % 2) * 4096:(w % 2) * 4096 + 4096]
            pslist = [ps5, ps6]
            for q in range(8):
                a, dch = q // 2, (q % 2) * 4
                def mkq(w=w, a=a, dch=dch, q=q):
                    ps = pslist[q % 2]
                    kcb2 = (xg if w % 2 == 0 else h2p)[:, 0:4096]
                    def fn(t):
                        r = None
                        for j in range(4):
                            dc = dch + j
                            r = t.transpose(ps[:, j * 128:(j + 1) * 128],
                                            kcb2[:, a * D + dc * 128: a * D + (dc + 1) * 128],
                                            idf[:])
                        return r
                    return fn
                wl = [("SD", at(f"kc{w}"))]
                if tposes < 4:
                    wl.append(("VV", at("ppdone")))
                if tposes >= 2:
                    wl.append(("VV", mark[f"ktv{tposes-2}"]))
                emit("T", mkq(), waits=wl, inc="TT", tag=f"ktq{tposes}")
                def mkev2(w=w, a=a, dch=dch, q=q):
                    ps = pslist[q % 2]
                    stg2 = winp[:, (w % 2) * 4096:(w % 2) * 4096 + 4096]
                    def fn(v):
                        r = None
                        for j in range(4):
                            dc = dch + j
                            r = v.tensor_copy(stg2[:, dc * 512 + a * 128: dc * 512 + (a + 1) * 128],
                                              ps[:, j * 128:(j + 1) * 128])
                        return r
                    return fn
                emit("V", mkev2(), waits=[("TT", at(f"ktq{tposes}"))], inc="VV",
                     tag=f"ktv{tposes}")
                tposes += 1
            emit("T", lambda t: t.nop(), waits=[("VV", at(f"ktv{tposes-1}"))],
                 inc="TT", tag=f"ktt{w}")
            dma("Y", "SD", ktd[w], stg,
                waits=[("VV", at(f"ktv{tposes-1}"))], tag=f"ktw{w}")

        def emit_gsweeps():
          for sw in range(2):
              gslc = [(ps4[:, 0:512], ps4[:, 512:1024]),
                      (ps4[:, 1024:1536], ps4[:, 1536:2048]),
                      (ps1[:], ps2[:]), (ps5[:], ps6[:])]
              for w in range(NW):
                  waits = []
                  if sw == 0 and w < 2:
                      waits.append(("SD", at(f"ktw{NW-1}")))
                  if w >= 2 or sw > 0:
                      pw = (sw, w - 2) if w >= 2 else (sw - 1, NW - 2 + w)
                      waits.append(("TT", at(f"gmm{pw[0]}_{pw[1]}")))
                  kcb = vwin[:, (w % 2) * 4096:(w % 2) * 4096 + 4096]
                  dma("G", "GD", kcb.rearrange("p (a d) -> p a d", d=D),
                      k_d[w * 512:(w + 1) * 512, :].rearrange("(a p) d -> p a d", p=128),
                      waits=waits, tag=f"gw{sw}_{w}")

                  def mkg(sw=sw, w=w):
                      kcb2 = vwin[:, (w % 2) * 4096:(w % 2) * 4096 + 4096]
                      def fn(t):
                          r = None
                          for a in range(4):
                              for dtl in range(4):
                                  dt = sw * 4 + dtl
                                  for jw in range(2):
                                      r = t.matmul(
                                          gslc[dtl][jw],
                                          kcb2[:, a * D + dt * 128:a * D + (dt + 1) * 128],
                                          kcb2[:, a * D + jw * 512:a * D + (jw + 1) * 512],
                                          start=(w == 0 and a == 0),
                                          stop=(w == NW - 1 and a == 3))
                          return r
                      return fn
                  wl = [("GD", at(f"gw{sw}_{w}"))]
                  if sw == 0 and w == 0:
                      wl.append(("VV", at("modev1")))
                      wl.append(("VV", at("g0_lna_rv")))
                  if sw == 1 and w == 0:
                      wl.append(("VV", at("gev0")))
                  emit("T", mkg(), waits=wl, inc="TT", tag=f"gmm{sw}_{w}")
              def mkgev(sw=sw):
                  def fn(v):
                      r = None
                      for dtl in range(4):
                          for jw in range(2):
                              r = v.tensor_copy(hbuf[:, dtl * 1024 + jw * 512: dtl * 1024 + (jw + 1) * 512],
                                                gslc[dtl][jw])
                      return r
                  return fn
              gevw = [("TT", at(f"gmm{sw}_{NW-1}"))]
              if sw == 1:
                  gevw.append(("SD", at("gdw0")))
              emit("V", mkgev(), waits=gevw, inc="VV", tag=f"gev{sw}")
              dma("Y", "SD", gdr[sw * 512:(sw + 1) * 512, :].rearrange("(dt p) dd -> p dt dd", p=128),
                  hbuf[:].rearrange("p (dt dd) -> p dt dd", dd=1024),
                  waits=[("VV", at(f"gev{sw}"))], tag=f"gdw{sw}")

        GSEL = 16384

        def ppap(slot, dc):
            return sb_pp[:, dc, slot:slot + 1]

        for g in range(ng):
            gp = f"g{g}"
            def emit_xla(a, gg=None):
                gg = g if gg is None else gg
                wl = []
                if gg == 0 and a == 0:
                    wl.append(("TT", at("ppt7")))
                if a >= 2:
                    wl.append(("TT", at(f"g{gg}_xtq{(a-2)*2+1}")))
                if gg > 0 and a < 2:
                    wl.append(("TT", at(f"g{gg-1}_xtq{4+a}")))
                dma("Y", "SD", xtm[:, a % 2, :],
                    x_d[gg * TG + a * 128:gg * TG + (a + 1) * 128, :],
                    waits=wl, tag=f"g{gg}_xla{a}")
            if g == 0:
                emit_xla(0)
                emit_xla(1)
            for q in range(8):
                a, dch = q // 2, (q % 2) * 4
                if q in (2, 4) :
                    emit_xla(a + 1)
                pslist = [ps5, ps6, ps1, ps2]
                def mkxq(g=g, a=a, dch=dch, q=q):
                    ps = pslist[q % 4]
                    def fn(t):
                        r = None
                        for j in range(4):
                            dc = dch + j
                            r = t.transpose(ps[:, j * 128:(j + 1) * 128],
                                            xtm[:, a % 2, dc * 128:(dc + 1) * 128], idf[:])
                        return r
                    return fn
                wl = [("SD", at(f"{gp}_xla{a}"))]
                if g > 0 and q == 0:
                    wl.append(("VV", at(f"g{g-1}_otv7")))
                if g == 0 and q == 0:
                    wl.append(("VV", mark["ktv127"]))
                if q >= 4:
                    wl.append(("VV", at(f"{gp}_xev{q-4}")))
                emit("T", mkxq(), waits=wl, inc="TT", tag=f"{gp}_xtq{q}")
                def mkxev(g=g, a=a, dch=dch, q=q):
                    ps = pslist[q % 4]
                    def fn(v):
                        r = None
                        for j in range(4):
                            dc = dch + j
                            r = v.tensor_copy(xg[:, dc * TG + a * 128: dc * TG + (a + 1) * 128],
                                              ps[:, j * 128:(j + 1) * 128])
                        return r
                    return fn
                wl = [("TT", at(f"{gp}_xtq{q}"))]
                if g > 0 and q == 0:
                    wl.append(("GD", at(f"g{g-1}_ost")))
                emit("V", mkxev(), waits=wl, inc="VV", tag=f"{gp}_xev{q}")

            def XG(dc):
                return xg[:, dc * TG:(dc + 1) * TG]

            def HB(dc):
                return hbuf[:, dc * TG:(dc + 1) * TG]

            def XQ(dc):
                return xsq[:, dc * TG:(dc + 1) * TG]

            def H2(dc):
                return h2p[:, dc * TG:(dc + 1) * TG]

            def Y2(dc):
                return y2b[:, dc * TG:(dc + 1) * TG]

            def ln_stats(pref, src_ap_fn, src_is_bf, eps, wait_v_tag, extra_t_waits=(),
                         chunk_tags=None):
                def mksq():
                    def fn(s):
                        r = None
                        for dc in range(DC):
                            r = s.activation(XQ(dc), src_ap_fn(dc), AF.Square)
                        return r
                    return fn
                emit("S", mksq(), waits=[("VV", at(wait_v_tag))], inc="SS",
                     tag=f"{pref}_sq")
                for dc in range(DC):
                    def mksx(dc=dc):
                        def fn(t):
                            return t.matmul(ps1[0:1, :], onb[:] if src_is_bf else onf[:],
                                            src_ap_fn(dc), start=(dc == 0),
                                            stop=(dc == DC - 1))
                        return fn
                    wl = list(extra_t_waits) if dc == 0 else []
                    if chunk_tags is not None:
                        wl.append(("VV", at(chunk_tags[dc])))
                    elif dc == 0:
                        wl.append(("VV", at(wait_v_tag)))
                    emit("T", mksx(dc), waits=wl, inc="TT", tag=f"{pref}_sx{dc}")
                def mkst():
                    def fn(t):
                        r = None
                        for dc in range(DC):
                            r = t.matmul(ps2[0:1, :], onb[:], XQ(dc),
                                         start=(dc == 0), stop=(dc == DC - 1))
                        return r
                    return fn
                emit("T", mkst(), waits=[("SS", at(f"{pref}_sq"))], inc="TT",
                     tag=f"{pref}_st")
                def mkv():
                    def fn(v):
                        v.tensor_copy(rowb[0:1, MU:MU + TG], ps1[0:1, :])
                        v.tensor_copy(rowb[0:1, E2:E2 + TG], ps2[0:1, :])
                        v.tensor_tensor(rowb[0:1, TMP:TMP + TG], rowb[0:1, MU:MU + TG],
                                        rowb[0:1, MU:MU + TG], ALU.mult)
                        v.tensor_tensor(rowb[0:1, E2:E2 + TG], rowb[0:1, E2:E2 + TG],
                                        rowb[0:1, TMP:TMP + TG], ALU.subtract)
                        return v.tensor_scalar_add(rowb[0:1, E2:E2 + TG],
                                                   rowb[0:1, E2:E2 + TG], eps)
                    return fn
                vwl = [("TT", at(f"{pref}_st"))]
                if pref == "g0_lna":
                    vwl.append(("GD", at("ppw2")))
                emit("V", mkv(), waits=vwl, inc="VV", tag=f"{pref}_v1")
                emit("S", lambda s: s.activation(rowb[0:1, TMP:TMP + TG],
                                                 rowb[0:1, E2:E2 + TG], AF.Sqrt),
                     waits=[("VV", at(f"{pref}_v1"))], inc="SS", tag=f"{pref}_s1")
                emit("V", lambda v: v.reciprocal(rowb[0:1, RR:RR + TG],
                                                 rowb[0:1, TMP:TMP + TG]),
                     waits=[("SS", at(f"{pref}_s1"))], inc="VV", tag=f"{pref}_v2")
                def mkrep():
                    def fn(t):
                        t.matmul(ps4[:, 0:512], o1r[:], rowb[0:1, MU:MU + TG],
                                 start=True, stop=True)
                        return t.matmul(ps4[:, 512:1024], o1r[:], rowb[0:1, RR:RR + TG],
                                        start=True, stop=True)
                    return fn
                emit("T", mkrep(), waits=[("VV", at(f"{pref}_v2"))], inc="TT",
                     tag=f"{pref}_rp")
                def mkrev():
                    def fn(v):
                        v.tensor_copy(reps[:, 0, :], ps4[:, 0:512])
                        return v.tensor_copy(reps[:, 1, :], ps4[:, 512:1024])
                    return fn
                emit("V", mkrev(), waits=[("TT", at(f"{pref}_rp"))], inc="VV",
                     tag=f"{pref}_rv")

            ln_stats(f"{gp}_lna", XG, False, 1e-5, f"{gp}_xev7",
                     extra_t_waits=([("VV", at(f"g{g-1}_b7d"))] if g > 0 else
                                    [("VV", at("ppdone"))]))

            if g == 0:
                emit_gsweeps()
                emit_staging_casts()
            for dc in range(DC):
                def mkh(g=g, dc=dc):
                    def fn(v):
                        v.tensor_tensor(scr[:], XG(dc), reps[:, 0, :], ALU.subtract)
                        v.tensor_tensor(scr[:], scr[:], reps[:, 1, :], ALU.mult)
                        return v.tensor_scalar(HB(dc), scr[:], ppap(PP_CA1, dc),
                                               ppap(PP_SA, dc), ALU.mult, ALU.add)
                    return fn
                hwl = []
                if dc == 0:
                    hwl = [("VV", at(f"{gp}_lna_rv")), ("VV", at("ppdone"))]
                    if g == 0:
                        hwl.append(("SD", at("gdw1")))
                emit("V", mkh(g, dc), waits=hwl, inc="VV", tag=f"{gp}_h{dc}")
            mark[f"{gp}_h"] = mark[f"{gp}_h{DC-1}"]

            gtgt = [ps4[:, 0:512], ps4[:, 512:1024], ps4[:, 1024:1536],
                    ps4[:, 1536:2048], ps1[:], ps2[:], ps5[:], ps6[:]]
            wl = [("SD", at("gdw1"))]
            if g > 0:
                wl.append(("TT", at(f"g{g-1}_at{NW-1}")))
            dma("Y", "SD", vwin[:].rearrange("p (c n) -> p c n", n=1024),
                gdr[:, :].rearrange("(c p) n -> p c n", p=128),
                waits=wl, tag=f"{gp}_gw")
            for dc in range(DC):
                def mkgh(g=g, dc=dc):
                    def fn(t):
                        r = None
                        for dt in range(DC):
                            r = t.matmul(gtgt[dt],
                                         vwin[:, dc * 1024 + dt * 128: dc * 1024 + (dt + 1) * 128],
                                         HB(dc), start=(dc == 0), stop=(dc == DC - 1))
                        return r
                    return fn
                wl = [("VV", at(f"{gp}_h{dc}"))]
                if dc == 0:
                    wl.append(("SD", at(f"{gp}_gw")))
                emit("T", mkgh(g, dc), waits=wl, inc="TT", tag=f"{gp}_ghc{dc}")
            mark[f"{gp}_gh"] = mark[f"{gp}_ghc{DC-1}"]
            def mkpm(g=g):
                def fn(v):
                    r = None
                    for dt in range(DC):
                        r = v.tensor_mul(XQ(dt), gtgt[dt], HB(dt))
                    return r
                return fn
            emit("V", mkpm(g), waits=[("TT", at(f"{gp}_gh"))], inc="VV",
                 tag=f"{gp}_pm1")
            if dbg and g == 0:
                dma("G", "GD", dbg_t["dbg_P"], xsq[:], waits=[("VV", at(f"{gp}_pm1"))],
                    tag="tap_P")
                dma("G", "GD", dbg_t["dbg_hh"], hbuf[:],
                    waits=[("VV", at(f"{gp}_h"))], tag="tap_hh")
            def mkn2(g=g):
                def fn(t):
                    r = None
                    for dc in range(DC):
                        r = t.matmul(ps1[0:1, :], on1b[:], XQ(dc),
                                     start=(dc == 0), stop=(dc == DC - 1))
                    return r
                return fn
            emit("T", mkn2(g), waits=[("VV", at(f"{gp}_pm1"))], inc="TT", tag=f"{gp}_n2")
            def mkn2v(g=g):
                def fn(v):
                    return v.reciprocal(rowb[0:1, TMP:TMP + TG], ps1[0:1, :])
                return fn
            emit("V", mkn2v(g), waits=[("TT", at(f"{gp}_n2"))], inc="VV", tag=f"{gp}_n2v")
            emit("S", lambda s: s.activation(rowb[0:1, N2:N2 + TG],
                                             rowb[0:1, TMP:TMP + TG], AF.Sqrt,
                                             scale=float(NBANK)),
                 waits=[("VV", at(f"{gp}_n2v"))], inc="SS", tag=f"{gp}_srt")
            emit("T", lambda t, g=g: t.matmul(ps2[:, :], o1r[:], rowb[0:1, N2:N2 + TG],
                                              start=True, stop=True),
                 waits=[("SS", at(f"{gp}_srt"))], inc="TT", tag=f"{gp}_srp")
            emit("V", lambda v, g=g: v.tensor_copy(reps[:, 0, :], ps2[:, :]),
                 waits=[("TT", at(f"{gp}_srp"))], inc="VV", tag=f"{gp}_sev")
            def mkhs(g=g):
                def fn(v):
                    r = None
                    for dc in range(DC):
                        r = v.tensor_tensor(HB(dc), HB(dc), reps[:, 0, :], ALU.mult)
                    return r
                return fn
            hswl = [("VV", at(f"{gp}_sev"))]
            if dbg and g == 0:
                hswl.append(("GD", at("tap_hh")))
                hswl.append(("GD", at("tap_P")))
            emit("V", mkhs(g), waits=hswl, inc="VV", tag=f"{gp}_hs")

            pslist = [ps1, ps2, ps5, ps6]
            for w in range(NW):
                ktb = winp[:, (w % 2) * 4096:(w % 2) * 4096 + 4096]
                wl = []
                if g == 0 and w < 2:
                    wl.append(("SD", at(f"ktw{NW-1}")))
                if g > 0 and w < 2:
                    wl.append(("TT", at(f"g{g-1}_wo15")))
                if w >= 2:
                    wl.append(("TT", at(f"{gp}_sc{(w-2)*4+3}")))
                dma("Y", "SD", ktb, ktd[w], waits=wl, tag=f"{gp}_ktw{w}")
                for ntl in range(4):
                    nt = w * 4 + ntl
                    def mksc(g=g, w=w, ntl=ntl, nt=nt):
                        ps = pslist[nt % 4]
                        ktb2 = winp[:, (w % 2) * 4096:(w % 2) * 4096 + 4096]
                        def fn(t):
                            r = None
                            for dc in range(DC):
                                r = t.matmul(ps[:],
                                             ktb2[:, dc * 512 + ntl * 128: dc * 512 + (ntl + 1) * 128],
                                             HB(dc), start=(dc == 0), stop=(dc == DC - 1))
                            return r
                        return fn
                    wl = [("SD", at(f"{gp}_ktw{w}")), ("VV", at(f"{gp}_hs"))]
                    if nt >= 4:
                        wl.append(("SS", at(f"{gp}_ge{nt-4}")))
                    elif nt == 0:
                        wl.append(("SS", at(f"{gp}_srt")))
                        wl.append(("VV", at(f"{gp}_sev")))
                    emit("T", mksc(g, w, ntl, nt), waits=wl, inc="TT", tag=f"{gp}_sc{nt}")
                    def mkge(g=g, nt=nt):
                        ps = pslist[nt % 4]
                        def fn(s):
                            return s.activation(big[:, nt * 512:(nt + 1) * 512], ps[:],
                                                AF.Gelu)
                        return fn
                    emit("S", mkge(g, nt), waits=[("TT", at(f"{gp}_sc{nt}"))],
                         inc="SS", tag=f"{gp}_ge{nt}")

            if dbg and g == 0:
                dma("G", "GD", dbg_t["dbg_gated"], big[:, 0:32768],
                    waits=[("SS", at(f"{gp}_ge{NCH-1}"))], tag="tap_gated")

            atgt = [ps4[:, 0:512], ps4[:, 512:1024], ps4[:, 1024:1536], ps4[:, 1536:2048],
                    ps1[:], ps2[:], ps5[:], ps6[:]]
            for w in range(NW):
                vb = vwin[:, (w % 2) * 4096:(w % 2) * 4096 + 4096]
                wl = [("GD", at("vbf3"))]
                if w < 2:
                    wl.append(("TT", at(f"{gp}_gh")))
                else:
                    wl.append(("TT", at(f"{gp}_at{w-2}")))
                dma("Y", "SD", vb.rearrange("p (a d) -> p a d", d=D),
                    vbf[w * 512:(w + 1) * 512, :].rearrange("(a p) d -> p a d", p=128),
                    waits=wl, tag=f"{gp}_vw{w}")
                def mkat(g=g, w=w):
                    vb2 = vwin[:, (w % 2) * 4096:(w % 2) * 4096 + 4096]
                    def fn(t):
                        r = None
                        for a in range(4):
                            nch = w * 4 + a
                            for dt in range(DC):
                                r = t.matmul(atgt[dt],
                                             vb2[:, a * D + dt * 128: a * D + (dt + 1) * 128],
                                             big[:, nch * 512:(nch + 1) * 512],
                                             start=(nch == 0), stop=(nch == NCH - 1))
                        return r
                    return fn
                wl = [("SD", at(f"{gp}_vw{w}"))]
                if w == 0:
                    wl.append(("SS", at(f"{gp}_ge{NCH-1}")))
                emit("T", mkat(g, w), waits=wl, inc="TT", tag=f"{gp}_at{w}")

            if g + 1 < ng:
                emit_xla(0, g + 1)
                emit_xla(1, g + 1)

            for dt in range(DC):
                def mka8(g=g, dt=dt):
                    def fn(v):
                        return v.scalar_tensor_tensor(XG(dt), atgt[dt], ppap(PP_GA, dt),
                                                      XG(dt), ALU.mult, ALU.add)
                    return fn
                wl = [("TT", at(f"{gp}_at{NW-1}"))] if dt == 0 else []
                emit("V", mka8(g, dt), waits=wl, inc="VV", tag=f"{gp}_a8_{dt}")
            mark[f"{gp}_a8"] = mark[f"{gp}_a8_{DC-1}"]
            if dbg and g == 0:
                dma("G", "GD", dbg_t["dbg_x2"], xg[:], waits=[("VV", at(f"{gp}_a8"))],
                    tag="tap_x2")

            ln_stats(f"{gp}_lnb", XG, False, 1e-5, f"{gp}_a8",
                     extra_t_waits=[],
                     chunk_tags=[f"{gp}_a8_{max(dc, 4)}" for dc in range(DC)])
            for dc in range(DC):
                def mkh2(g=g, dc=dc):
                    def fn(v):
                        v.tensor_tensor(scr[:], XG(dc), reps[:, 0, :], ALU.subtract)
                        v.tensor_tensor(scr[:], scr[:], reps[:, 1, :], ALU.mult)
                        return v.tensor_scalar(H2(dc), scr[:], ppap(PP_CB1, dc),
                                               ppap(PP_SB, dc), ALU.mult, ALU.add)
                    return fn
                wl = []
                if dc == 0:
                    wl = [("VV", at(f"{gp}_lnb_rv"))]
                    if g > 0:
                        wl.append(("VV", at(f"g{g-1}_otv7")))
                emit("V", mkh2(g, dc), waits=wl, inc="VV", tag=f"{gp}_h2_{dc}")
            mark[f"{gp}_h2"] = mark[f"{gp}_h2_{DC-1}"]
            if dbg and g == 0:
                dma("G", "GD", dbg_t["dbg_h2"], h2p[:], waits=[("VV", at(f"{gp}_h2"))],
                    tag="tap_h2")

            rtgt = [ps5[0:1, :], ps6[0:1, :], ps1[0:1, :], ps2[0:1, :]]
            for dc in range(DC):
                def mkrt(g=g, dc=dc):
                    def fn(t):
                        r = None
                        for e in range(E):
                            r = t.matmul(rtgt[e], sb_wr[:, dc, e:e + 1], H2(dc),
                                         start=(dc == 0), stop=(dc == DC - 1))
                        return r
                    return fn
                wl = [("VV", at(f"{gp}_h2_{dc}"))]
                if dc == 0:
                    wl.append(("GD", at("wr_ld")))
                emit("T", mkrt(g, dc), waits=wl, inc="TT", tag=f"{gp}_rtc{dc}")
            mark[f"{gp}_rt"] = mark[f"{gp}_rtc{DC-1}"]

            def mktn(g=g):
                def fn(v):
                    for e in range(E):
                        v.tensor_copy(rowb[0:1, LCAT + e * TG:LCAT + (e + 1) * TG],
                                      rtgt[e])
                    L = lambda e: rowb[0:1, LCAT + e * TG:LCAT + (e + 1) * TG]
                    M = lambda e: rowb[0:1, MASK + e * TG:MASK + (e + 1) * TG]
                    tmp = rowb[0:1, TMP:TMP + TG]
                    mx = rowb[0:1, MU:MU + TG]
                    v.tensor_tensor(tmp, L(0), L(1), ALU.max)
                    v.tensor_tensor(mx, L(2), L(3), ALU.max)
                    v.tensor_tensor(mx, tmp, mx, ALU.max)
                    v.tensor_tensor(M(0), L(0), mx, ALU.is_ge)
                    rem = rowb[0:1, RR:RR + TG]
                    v.tensor_scalar(rem, M(0), -1.0, 1.0, ALU.mult, ALU.add)
                    v.tensor_tensor(M(1), L(1), mx, ALU.is_ge)
                    v.tensor_tensor(M(1), M(1), rem, ALU.mult)
                    v.tensor_scalar(tmp, M(1), -1.0, 1.0, ALU.mult, ALU.add)
                    v.tensor_tensor(rem, rem, tmp, ALU.mult)
                    v.tensor_tensor(M(2), L(2), mx, ALU.is_ge)
                    v.tensor_tensor(M(2), M(2), rem, ALU.mult)
                    v.tensor_scalar(tmp, M(2), -1.0, 1.0, ALU.mult, ALU.add)
                    v.tensor_tensor(rem, rem, tmp, ALU.mult)
                    return v.tensor_copy(M(3), rem)
                return fn
            emit("V", mktn(g), waits=[("TT", at(f"{gp}_rt"))], inc="VV", tag=f"{gp}_tn")
            dma("Y", "SD", msc2.rearrange("(one t) -> one t", one=1),
                rowb[0:1, MASK:MASK + E * TG],
                waits=[("VV", at(f"{gp}_tn"))], tag=f"{gp}_m40")
            dma("Y", "SD", sb_m4[:], msc2.rearrange("(e t) -> e t", e=E),
                waits=[("SD", at(f"{gp}_m40"))], tag=f"{gp}_m4")
            emit("V", lambda v, g=g: v.tensor_copy(sb_m4b[:], sb_m4[:]),
                 waits=[("SD", at(f"{gp}_m4"))], inc="VV", tag=f"{gp}_m4b")

            mtgt = [ps6[:], ps5[:], ps1[:], ps2[:]]
            def mkmr(g=g):
                def fn(t):
                    r = None
                    for e in range(E):
                        r = t.matmul(mtgt[e], o1r[:],
                                     rowb[0:1, MASK + e * TG:MASK + (e + 1) * TG],
                                     start=True, stop=True)
                    return r
                return fn
            emit("T", mkmr(g), waits=[("VV", at(f"{gp}_tn")), ("VV", at(f"{gp}_a8"))],
                 inc="TT", tag=f"{gp}_mrt")
            for e in range(E):
                def mkh2m(g=g, e=e):
                    def fn(v):
                        r = None
                        for dc in range(DC):
                            r = v.tensor_mul(big[:, (e * DC + dc) * 512:(e * DC + dc + 1) * 512],
                                             H2(dc), mtgt[e])
                        return r
                    return fn
                wl = [("TT", at(f"{gp}_mrt"))] if e == 0 else []
                if dbg and g == 0 and e == 0:
                    wl.append(("GD", at("tap_gated")))
                emit("V", mkh2m(g, e), waits=wl, inc="VV", tag=f"{gp}_mr{e}")

            pslist2 = [ps6, ps5, ps1, ps2]

            def emit_w1w(it, half):
                w1b = winp[:, (it % 2) * 4096:(it % 2) * 4096 + 4096]
                wl = [("GD", at("w1bf3_1"))]
                if it < 2:
                    wl.append(("TT", at(f"{gp}_sc{NCH-1}")))
                else:
                    wl.append(("TT", at(f"{gp}_he{it-2}")))
                es = (0, 1) if half == 0 else (2, 3)
                qeng, qsem = ("Y", "SD") if half == 0 else ("S", "SC")
                for e in es:
                    dma(qeng, qsem,
                        w1b[:, e * 1024:(e + 1) * 1024].rearrange("p (c i) -> p c i", i=128),
                        w1bf[e, :, it * 128:(it + 1) * 128].rearrange("(c p) i -> p c i", p=128),
                        waits=(wl if e in es[:1] else []), tag=f"{gp}_w1w{it}_{e}")
            emit_w1w(0, 1)
            emit_w1w(1, 1)
            for it in range(IT):
                emit_w1w(it, 0)
                for e in range(E):
                    def mkhee(g=g, it=it, e=e):
                        ps = pslist2[it % 4]
                        w1b2 = winp[:, (it % 2) * 4096:(it % 2) * 4096 + 4096]
                        def fn(t):
                            r = None
                            for dc in range(DC):
                                r = t.matmul(ps[:],
                                             w1b2[:, (e * DC + dc) * 128:(e * DC + dc + 1) * 128],
                                             big[:, (e * DC + dc) * 512:(e * DC + dc + 1) * 512],
                                             start=(e == 0 and dc == 0), stop=False)
                            return r
                        return fn
                    wl2 = [("VV", at(f"{gp}_mr{e}"))]
                    if e == 0 and it < 4:
                        wl2 = [("VV", at(f"{gp}_mr{it}"))]
                    if e < 2:
                        wl2.append(("SD", at(f"{gp}_w1w{it}_{e}")))
                    else:
                        wl2.append(("SC", at(f"{gp}_w1w{it}_{e}")))
                    if e == 0 and it >= 4:
                        wl2.append(("SS", at(f"{gp}_fg{it-4}")))
                    emit("T", mkhee(g, it, e), waits=wl2, inc="TT", tag=f"{gp}_hee{it}_{e}")
                def mkheb(g=g, it=it):
                    ps = pslist2[it % 4]
                    def fn(t):
                        return t.matmul(ps[:], sb_b1[:, it * 128:(it + 1) * 128],
                                        sb_m4b[:], start=False, stop=True)
                    return fn
                emit("T", mkheb(g, it), waits=[("VV", at(f"{gp}_m4b")), ("GD", at("b1_ld"))],
                     inc="TT", tag=f"{gp}_he{it}")
                if it + 2 < IT:
                    emit_w1w(it + 2, 1)
                def mkfg(g=g, it=it):
                    ps = pslist2[it % 4]
                    def fn(s):
                        return s.activation(big[:, GSEL + it * 512:GSEL + (it + 1) * 512],
                                            ps[:], AF.Gelu)
                    return fn
                fgw = [("TT", at(f"{gp}_he{it}"))]
                if dbg and g == 0 and it == 0:
                    fgw.append(("GD", at("tap_gated")))
                emit("S", mkfg(g, it), waits=fgw, inc="SS", tag=f"{gp}_fg{it}")

            for hh in range(2):
                for ic4 in range(8):
                    wi = hh * 8 + ic4
                    wob = winp[:, (wi % 2) * 2048 + 4096 * 0:(wi % 2) * 2048 + 2048]
                    wl = []
                    if wi < 2:
                        wl.append(("TT", at(f"{gp}_he{IT-2}")))
                        wl.append(("GD", at("wobf1")))
                    else:
                        wl.append(("TT", at(f"{gp}_wo{wi-2}")))
                    dma("Y", "SD", wob.rearrange("p (a h) -> p a h", h=512),
                        wobf[ic4 * 512:(ic4 + 1) * 512, hh * 512:(hh + 1) * 512]
                        .rearrange("(a p) h -> p a h", p=128),
                        waits=wl, tag=f"{gp}_wow{wi}")
                    def mkwo(g=g, hh=hh, ic4=ic4, wi=wi):
                        wob2 = winp[:, (wi % 2) * 2048:(wi % 2) * 2048 + 2048]
                        def fn(t):
                            r = None
                            for a in range(4):
                                ic = ic4 * 4 + a
                                for htl in range(4):
                                    r = t.matmul(ps4[:, htl * 512:(htl + 1) * 512],
                                                 wob2[:, a * 512 + htl * 128: a * 512 + (htl + 1) * 128],
                                                 big[:, GSEL + ic * 512:GSEL + (ic + 1) * 512],
                                                 start=(ic == 0), stop=(ic == IT - 1))
                            return r
                        return fn
                    wl2 = [("SD", at(f"{gp}_wow{wi}"))]
                    if wi == 0:
                        wl2.append(("SS", at(f"{gp}_fg{IT-1}")))
                        wl2.append(("VV", at(f"{gp}_lnb_rv")))
                    if ic4 == 0 and hh == 1:
                        wl2.append(("SS", at(f"{gp}_ye0")))
                    emit("T", mkwo(g, hh, ic4, wi), waits=wl2, inc="TT",
                         tag=f"{gp}_wo{wi}")
                def mkye(g=g, hh=hh):
                    def fn(s):
                        r = None
                        for htl in range(4):
                            dt = hh * 4 + htl
                            r = s.activation(Y2(dt), ps4[:, htl * 512:(htl + 1) * 512],
                                             AF.Identity, bias=ppap(PP_BO, dt))
                        return r
                    return fn
                emit("S", mkye(g, hh), waits=[("TT", at(f"{gp}_wo{hh*8+7}"))],
                     inc="SS", tag=f"{gp}_ye{hh}")
                def mkya(g=g, hh=hh):
                    def fn(v):
                        r = None
                        for htl in range(4):
                            dt = hh * 4 + htl
                            r = v.tensor_tensor(Y2(dt), Y2(dt), H2(dt), ALU.add)
                        return r
                    return fn
                emit("V", mkya(g, hh), waits=[("SS", at(f"{gp}_ye{hh}"))], inc="VV",
                     tag=f"{gp}_ya{hh}")

            ln_stats(f"{gp}_lnc", Y2, True, 1e-12, f"{gp}_ya1",
                     extra_t_waits=[],
                     chunk_tags=[f"{gp}_ya{0 if dc < 4 else 1}" for dc in range(DC)])

            if dbg and g == 0:
                pass
            for dc in range(DC):
                def mkb7(g=g, dc=dc):
                    def fn(v):
                        v.tensor_tensor(scr[:], Y2(dc), reps[:, 0, :], ALU.subtract)
                        v.tensor_tensor(scr[:], scr[:], reps[:, 1, :], ALU.mult)
                        v.tensor_scalar(scr[:], scr[:], ppap(PP_GGB, dc),
                                        ppap(PP_BGB, dc), ALU.mult, ALU.add)
                        return v.tensor_tensor(H2(dc), scr[:], XG(dc), ALU.add)
                    return fn
                b7w = []
                if dc == 0:
                    b7w = [("VV", at(f"{gp}_lnc_rv"))]
                    if dbg and g == 0:
                        b7w.append(("GD", at("tap_h2")))
                emit("V", mkb7(g, dc), waits=b7w, inc="VV", tag=f"{gp}_b7_{dc}")
            mark[f"{gp}_b7d"] = mark[f"{gp}_b7_{DC-1}"]

            pslist3 = [ps5, ps6, ps1, ps2]
            for q in range(8):
                a, dch = q // 2, (q % 2) * 4
                def mkot(g=g, a=a, dch=dch, q=q):
                    ps = pslist3[q % 4]
                    def fn(t):
                        r = None
                        for j in range(4):
                            dc = dch + j
                            r = t.transpose(ps[:, j * 128:(j + 1) * 128],
                                            h2p[:, dc * TG + a * 128: dc * TG + (a + 1) * 128],
                                            idf[:])
                        return r
                    return fn
                wl = [("VV", at(f"{gp}_b7_{3 if (q % 2) == 0 else 7}"))]
                if q >= 4:
                    wl.append(("VV", at(f"{gp}_otv{q-4}")))
                emit("T", mkot(g, a, dch, q), waits=wl, inc="TT", tag=f"{gp}_ot{q}")
                def mkotv(g=g, a=a, dch=dch, q=q):
                    ps = pslist3[q % 4]
                    def fn(v):
                        r = None
                        for j in range(4):
                            dc = dch + j
                            r = v.tensor_copy(xg[:, a * D + dc * 128: a * D + (dc + 1) * 128],
                                              ps[:, j * 128:(j + 1) * 128])
                        return r
                    return fn
                owl = [("TT", at(f"{gp}_ot{q}"))]
                if dbg and g == 0 and q == 0:
                    owl.append(("GD", at("tap_x2")))
                emit("V", mkotv(g, a, dch, q), waits=owl, inc="VV", tag=f"{gp}_otv{q}")
            dma("G", "GD", out_d[g * TG:(g + 1) * TG, :].rearrange("(a p) d -> p a d", p=128),
                xg[:].rearrange("p (a d) -> p a d", d=D),
                waits=[("VV", at(f"{gp}_otv7"))], tag=f"{gp}_ost")

        if dbg:
            barrier = [(sm, cnt[sm]) for sm in ("TT", "VV", "SS")]
            dma("G", "GD", dbg_t["dbg_h"], hbuf[:], waits=barrier)
            dma("G", "GD", dbg_t["dbg_y2"], y2b[:], waits=barrier)
            dma("G", "GD", dbg_t["dbg_gsel"], big[:, GSEL:GSEL + 16384], waits=barrier)
            dma("G", "GD", dbg_t["dbg_h2m"], big[:, 0:16384], waits=barrier)
            dma("G", "GD", dbg_t["dbg_rowb"], rowb[:], waits=barrier)
            dma("G", "GD", dbg_t["dbg_pp"], sb_pp[:], waits=barrier)
            dma("G", "GD", dbg_t["dbg_c"], sb_c[:], waits=barrier)
            dma("G", "GD", dbg_t["dbg_gdr"], gdr, waits=barrier)
            dma("G", "GD", dbg_t["dbg_ktd"], ktd, waits=barrier)

        final_gd = cnt["GD"]
        final_sd = cnt["SD"]
        emit("G", lambda g: g.wait_ge(sem["GD"], final_gd), inc=None)
        emit("Y", lambda y: y.wait_ge(sem["SD"], final_sd), inc=None)

        block = en(nc.Block())

        def replay(EE, key):
            lastw = {}
            for w, fn, inc, n in ops[key]:
                for s, v in w:
                    if lastw.get(s, -1) < v:
                        EE.wait_ge(sem[s], v)
                        lastw[s] = v
                r = fn(EE)
                if inc is not None:
                    if r is None:
                        EE.sem_inc(sem[inc], n)
                    else:
                        r.then_inc(sem[inc], n)

        @block.tensor
        def _(t):
            replay(t, "T")

        @block.vector
        def _(v):
            replay(v, "V")

        @block.scalar
        def _(s):
            replay(s, "S")

        @block.sync
        def _(y):
            replay(y, "Y")

        @block.gpsimd
        def _(gg):
            replay(gg, "G")

    return nc


_NC_CACHE = None


def kernel(L0_attn_q, L0_attn_k, L0_attn_v, L0_ffn_W_router, L0_ffn_W1, L0_ffn_b1,
           L0_ffn_Wo, L0_ffn_bo, L0_ffn_ln_gamma, L0_ffn_ln_beta, L0_c, L0_wmod):
    global _NC_CACHE
    if _NC_CACHE is None:
        _NC_CACHE = build_nc()
    nc = _NC_CACHE
    x = np.ascontiguousarray(np.asarray(L0_attn_q, dtype=np.float32))
    com = {
        "k": np.ascontiguousarray(np.asarray(L0_attn_k, np.float32)),
        "v": np.ascontiguousarray(np.asarray(L0_attn_v, np.float32)),
        "wr": np.ascontiguousarray(np.asarray(L0_ffn_W_router, np.float32)),
        "w1": np.ascontiguousarray(np.asarray(L0_ffn_W1, np.float32)),
        "b1": np.ascontiguousarray(np.asarray(L0_ffn_b1, np.float32)),
        "wo": np.ascontiguousarray(np.asarray(L0_ffn_Wo, np.float32)),
        "bo": np.ascontiguousarray(np.asarray(L0_ffn_bo, np.float32)),
        "gam": np.ascontiguousarray(np.asarray(L0_ffn_ln_gamma, np.float32)),
        "bet": np.ascontiguousarray(np.asarray(L0_ffn_ln_beta, np.float32)),
        "wmod": np.ascontiguousarray(np.asarray(L0_wmod, np.float32)),
    }
    c = np.ascontiguousarray(np.asarray(L0_c, np.float32))
    in_maps = []
    for i in range(8):
        m = dict(com)
        m["x"] = np.ascontiguousarray(x[i])
        m["c"] = np.ascontiguousarray(c[i:i + 1])
        in_maps.append(m)
    res = run_bass_kernel_spmd(nc, in_maps, core_ids=list(range(8)))
    return np.stack([r["out"] for r in res.results], axis=0)


# version 15
# speedup vs baseline: 1.0309x; 1.0309x over previous
import numpy as np
import concourse.bass as bass
import concourse.mybir as mybir
from concourse.bass_utils import run_bass_kernel_spmd

F32 = mybir.dt.float32
BF16 = mybir.dt.bfloat16
AF = mybir.ActivationFunctionType
ALU = mybir.AluOpType

T = 4096
D = 1024
DC = 8
TG = 512
NG = T // TG
NBANK = 8192
NCH = NBANK // 128
NW = 16
I_SEL = 4096
IT = 32
E = 4
SQRT_N = float(np.sqrt(np.float32(NBANK)))

MU, RR, E2, N2, TMP = 0, 512, 1024, 1536, 2048
LCAT, MASK = 2560, 4608
ROWLEN = 6656
MODR = 0


def build_nc(ng=NG, dbg=False):
    nc = bass.Bass("TRN2", target_bir_lowering=False)

    x_d = nc.dram_tensor("x", [T, D], F32, kind="ExternalInput").ap()
    c_d = nc.dram_tensor("c", [1, D], F32, kind="ExternalInput").ap()
    k_d = nc.dram_tensor("k", [NBANK, D], F32, kind="ExternalInput").ap()
    v_d = nc.dram_tensor("v", [NBANK, D], F32, kind="ExternalInput").ap()
    wr_d = nc.dram_tensor("wr", [D, E], F32, kind="ExternalInput").ap()
    w1_d = nc.dram_tensor("w1", [E, D, I_SEL], F32, kind="ExternalInput").ap()
    b1_d = nc.dram_tensor("b1", [E, I_SEL], F32, kind="ExternalInput").ap()
    wo_d = nc.dram_tensor("wo", [I_SEL, D], F32, kind="ExternalInput").ap()
    bo_d = nc.dram_tensor("bo", [D], F32, kind="ExternalInput").ap()
    gam_d = nc.dram_tensor("gam", [D], F32, kind="ExternalInput").ap()
    bet_d = nc.dram_tensor("bet", [D], F32, kind="ExternalInput").ap()
    wmod_d = nc.dram_tensor("wmod", [D, 6 * D], F32, kind="ExternalInput").ap()
    out_d = nc.dram_tensor("out", [T, D], F32, kind="ExternalOutput").ap()

    vbf = nc.dram_tensor("vbf", [NBANK, D], BF16).ap()
    w1bf = nc.dram_tensor("w1bf", [E, D, I_SEL], BF16).ap()
    wobf = nc.dram_tensor("wobf", [I_SEL, D], BF16).ap()
    ktd = nc.dram_tensor("ktd", [NW, 128, DC * 512], BF16).ap()
    gdr = nc.dram_tensor("gdr", [D, D], BF16).ap()
    mkind = dict(kind="ExternalOutput") if dbg else {}
    msc1 = nc.dram_tensor("msc1", [E * TG], F32, **mkind).ap()
    msc2 = nc.dram_tensor("msc2", [E * TG], F32, **mkind).ap()
    if dbg:
        dbg_t = {}
        for nm, shp, dt in [("dbg_h", [128, 4096], F32), ("dbg_x2", [128, 4096], F32),
                            ("dbg_h2", [128, 4096], F32), ("dbg_gated", [128, 32768], F32),
                            ("dbg_y2", [128, 4096], F32), ("dbg_gsel", [128, 16384], F32),
                            ("dbg_h2m", [128, 16384], F32), ("dbg_rowb", [1, ROWLEN], F32),
                            ("dbg_pp", [128, DC, 16], F32), ("dbg_c", [128, DC], F32),
                            ("dbg_gdr", [D, D], F32), ("dbg_ktd", [NW, 128, DC * 512], F32),
                            ("dbg_P", [128, 4096], F32), ("dbg_hh", [128, 4096], F32)]:
            dbg_t[nm] = nc.dram_tensor(nm, shp, dt, kind="ExternalOutput").ap()

    from contextlib import ExitStack
    with ExitStack() as st:
        en = st.enter_context
        sb_pp = en(nc.sbuf_tensor("sb_pp", [128, DC, 16], F32))
        sb_c = en(nc.sbuf_tensor("sb_c", [128, DC], F32))
        sb_wr = en(nc.sbuf_tensor("sb_wr", [128, DC, E], F32))
        sb_b1 = en(nc.sbuf_tensor("sb_b1", [E, I_SEL], BF16))
        sb_m4 = en(nc.sbuf_tensor("sb_m4", [E, TG], F32))
        sb_m4b = en(nc.sbuf_tensor("sb_m4b", [E, TG], BF16))
        sb_l4 = en(nc.sbuf_tensor("sb_l4", [E, TG], F32))
        rowb = en(nc.sbuf_tensor("rowb", [1, ROWLEN], F32))
        idf = en(nc.sbuf_tensor("idf", [128, 128], F32))
        idb = en(nc.sbuf_tensor("idb", [128, 128], BF16))
        iot = en(nc.sbuf_tensor("iot", [128, 128], F32))
        onf = en(nc.sbuf_tensor("onf", [128, 1], F32))
        onb = en(nc.sbuf_tensor("onb", [128, 1], BF16))
        on1b = en(nc.sbuf_tensor("on1b", [128, 1], BF16))
        o1r = en(nc.sbuf_tensor("o1r", [1, 128], F32))
        xtm = en(nc.sbuf_tensor("xtm", [128, 2, 1024], F32))
        xg = en(nc.sbuf_tensor("xg", [128, DC * TG], F32))
        hbuf = en(nc.sbuf_tensor("hbuf", [128, DC * TG], BF16))
        xsq = en(nc.sbuf_tensor("xsq", [128, DC * TG], BF16))
        reps = en(nc.sbuf_tensor("reps", [128, 2, TG], F32))
        scr = en(nc.sbuf_tensor("scr", [128, TG], F32))
        big = en(nc.sbuf_tensor("big", [128, 65536 // 2], BF16))
        winp = en(nc.sbuf_tensor("winp", [128, 8192], BF16))
        vwin = en(nc.sbuf_tensor("vwin", [128, 8192], BF16))
        h2p = en(nc.sbuf_tensor("h2p", [128, DC * TG], F32))
        y2b = en(nc.sbuf_tensor("y2b", [128, DC * TG], BF16))

        ps1 = en(nc.psum_tensor("ps1", [128, 512], F32))
        ps2 = en(nc.psum_tensor("ps2", [128, 512], F32))
        ps5 = en(nc.psum_tensor("ps5", [128, 512], F32))
        ps6 = en(nc.psum_tensor("ps6", [128, 512], F32))
        ps4 = en(nc.psum_tensor("ps4", [128, 2048], F32))

        sem = {s: en(nc.semaphore(s)) for s in ["SD", "GD", "TT", "VV", "SS", "SC"]}

        ops = {e: [] for e in ["T", "V", "S", "Y", "G"]}
        cnt = {s: 0 for s in sem}
        mark = {}

        def emit(eng, fn, waits=(), inc=None, n=1, tag=None):
            w = [(s, v) for (s, v) in waits if v > 0]
            ops[eng].append((w, fn, inc, n))
            if inc is not None:
                cnt[inc] += n
            if tag is not None:
                mark[tag] = cnt[inc]

        def at(tag):
            return mark[tag]

        def dma(eng, sdsem, out, in_, waits=(), tag=None, slow=False):
            def fn(EE):
                if slow:
                    return EE.dma_start(out, in_, allow_slow_non_contiguous=True)
                return EE.dma_start(out, in_)
            emit(eng, fn, waits, inc=sdsem, n=16, tag=tag)

        emit("G", lambda g: g.iota(iot[:], [[-1, 128]], channel_multiplier=1,
                                   allow_small_or_imprecise_dtypes=True),
             inc="GD", n=1, tag="iota")
        emit("V", lambda v: v.tensor_single_scalar(idf[:], iot[:], 0.0, ALU.is_equal),
             waits=[("GD", at("iota"))], inc="VV", tag="idf")
        emit("V", lambda v: v.tensor_copy(idb[:], idf[:]), inc="VV", tag="idb")
        emit("V", lambda v: v.memset(onf[:], 1.0 / 1024.0), inc="VV")
        emit("V", lambda v: v.memset(onb[:], 1.0 / 1024.0), inc="VV")
        emit("V", lambda v: v.memset(on1b[:], 1.0), inc="VV")
        emit("V", lambda v: v.memset(o1r[:], 1.0), inc="VV", tag="ones")

        dma("G", "GD", sb_c[:].rearrange("p c -> p c"),
            c_d.rearrange("one (c p) -> p (one c)", p=128), tag="c_ld", slow=True)
        dma("G", "GD", sb_wr[:].rearrange("p c e -> p c e"),
            wr_d.rearrange("(c p) e -> p c e", p=128), tag="wr_ld", slow=True)
        dma("G", "GD", sb_b1[:], b1_d, tag="b1_ld")


        emit("S", lambda s: s.activation(sb_c[:], sb_c[:], AF.Silu),
             waits=[("GD", at("c_ld"))], inc="SS", tag="siluc")

        modslc = [ps4[0:1, 0:512], ps4[0:1, 512:1024], ps4[0:1, 1024:1536],
                  ps4[0:1, 1536:2048], ps1[0:1, :], ps2[0:1, :]]
        for jh in range(2):
            for dc in range(DC):
                wv = (xg if dc % 2 == 0 else h2p)[:, 0:3072]
                waits = [("SS", at("siluc"))]
                seq = jh * DC + dc
                if seq >= 2:
                    pjh, pdc = divmod(seq - 2, DC)
                    waits.append(("TT", at(f"modmm{pjh}_{pdc}")))
                dma("Y", "SD", wv,
                    wmod_d[dc * 128:(dc + 1) * 128, jh * 3072:(jh + 1) * 3072],
                    waits=waits, tag=f"modw{jh}_{dc}")

                def mk(jh=jh, dc=dc):
                    def fn(t):
                        r = None
                        src = (xg if dc % 2 == 0 else h2p)
                        for jw in range(6):
                            wv2 = src[:, jw * 512:(jw + 1) * 512]
                            r = t.matmul(modslc[jw], sb_c[:, dc:dc + 1], wv2,
                                         start=(dc == 0), stop=(dc == DC - 1))
                        return r
                    return fn
                mmw = [("SD", at(f"modw{jh}_{dc}"))]
                if jh == 1 and dc == 0:
                    mmw.append(("VV", at("modev0")))
                emit("T", mk(), waits=mmw, inc="TT", tag=f"modmm{jh}_{dc}")
            def mkev(jh=jh):
                def fn(v):
                    r = None
                    for jw in range(6):
                        r = v.tensor_copy(rowb[0:1, jh * 3072 + jw * 512: jh * 3072 + (jw + 1) * 512],
                                          modslc[jw])
                    return r
                return fn
            emit("V", mkev(), waits=[("TT", at(f"modmm{jh}_{DC-1}"))], inc="VV",
                 tag=f"modev{jh}")

        dma("G", "GD", xg[0:1, 2048:3072],
            gam_d.rearrange("(one d) -> one d", one=1),
            waits=[("TT", at("modmm1_6"))], tag="gam_ld")
        dma("G", "GD", xg[0:1, 3072:4096],
            bet_d.rearrange("(one d) -> one d", one=1),
            waits=[("TT", at("modmm1_6"))], tag="bet_ld")

        def R(slot, off=0, ln=1024):
            return rowb[0:1, slot * 1024 + off: slot * 1024 + off + ln]

        def mkderive2():
            def fn(v):
                v.tensor_scalar_add(R(1), R(1), 1.0)
                v.tensor_scalar_add(R(4), R(4), 1.0)
                for sl in (2, 5):
                    tmp = xg[0:1, 0:1024]
                    tmp2 = xg[0:1, 1024:2048]
                    v.tensor_single_scalar(tmp, R(sl), 0.0, ALU.is_ge)
                    v.tensor_scalar(tmp, tmp, 2.0, -1.0, ALU.mult, ALU.add)
                    v.tensor_scalar_mul(tmp2, R(sl), -1.0)
                    v.tensor_tensor(tmp2, R(sl), tmp2, ALU.max)
                    v.tensor_scalar_max(tmp2, tmp2, 0.25)
                    v.tensor_tensor(R(sl), tmp, tmp2, ALU.mult)
                gmr = xg[0:1, 2048:3072]
                btr = xg[0:1, 3072:4096]
                v.tensor_tensor(gmr, gmr, R(5), ALU.mult)
                v.tensor_tensor(btr, btr, R(5), ALU.mult)
                return v.engine_nop()
            return fn
        emit("V", mkderive2(),
             waits=[("VV", at("modev1")), ("GD", at("bet_ld"))], inc="VV", tag="derive")

        dma("G", "GD", xtm[0:5, 0, :], rowb[0:1, 0:5 * 1024],
            waits=[("VV", at("derive"))], tag="ppw0")
        dma("G", "GD", xtm[5:7, 0, :], xg[0:1, 2048:4096],
            waits=[("VV", at("derive"))], tag="ppw1")
        dma("G", "GD", xtm[7:8, 0, :],
            bo_d.rearrange("(one d) -> one d", one=1), tag="ppw2")
        for dc in range(DC):
            def mkppt(dc=dc):
                def fn(t):
                    return t.transpose(ps6[:, 0:8],
                                       xtm[0:8, 0, dc * 128:(dc + 1) * 128], idf[0:8, 0:8])
                return fn
            wl = [("GD", at("ppw2")), ("TT", at("modmm1_7"))]
            if dc > 0:
                wl.append(("VV", at(f"ppv{dc-1}")))
            emit("T", mkppt(dc), waits=wl, inc="TT", tag=f"ppt{dc}")
            emit("V", lambda v, dc=dc: v.tensor_copy(sb_pp[:, dc, 0:8], ps6[:, 0:8]),
                 waits=[("TT", at(f"ppt{dc}"))], inc="VV", tag=f"ppv{dc}")
        emit("V", lambda v: v.engine_nop(), waits=[("VV", at(f"ppv{DC-1}"))],
             inc="VV", tag="ppdone")
        def emit_staging_casts():
            for j in range(4):
                dma("G", "GD", vbf[j * 2048:(j + 1) * 2048, :], v_d[j * 2048:(j + 1) * 2048, :],
                    tag=f"vbf{j}")
            for e in range(E):
                for h in range(2):
                    dma("G", "GD", w1bf[e, h * 512:(h + 1) * 512, :],
                        w1_d[e, h * 512:(h + 1) * 512, :], tag=f"w1bf{e}_{h}")
            for h in range(2):
                dma("G", "GD", wobf[h * 2048:(h + 1) * 2048, :], wo_d[h * 2048:(h + 1) * 2048, :],
                    tag=f"wobf{h}")
        PP_SA, PP_CA1, PP_GA, PP_SB, PP_CB1, PP_GGB, PP_BGB, PP_BO = 0, 1, 2, 3, 4, 5, 6, 7

        tposes = 0
        for w in range(NW):
            kcb = (xg if w % 2 == 0 else h2p)[:, 0:4096]
            waits = [("TT", at("modmm1_7")), ("GD", at("ppw1"))]
            if w < 2:
                waits.append(("VV", at("derive")))
            if w >= 2:
                waits.append(("TT", at(f"ktt{w-2}")))
            dma("Y", "SD", kcb.rearrange("p (a d) -> p a d", d=D),
                k_d[w * 512:(w + 1) * 512, :].rearrange("(a p) d -> p a d", p=128),
                waits=waits, tag=f"kc{w}")
            stg = winp[:, (w % 2) * 4096:(w % 2) * 4096 + 4096]
            pslist = [ps5, ps6]
            for q in range(8):
                a, dch = q // 2, (q % 2) * 4
                def mkq(w=w, a=a, dch=dch, q=q):
                    ps = pslist[q % 2]
                    kcb2 = (xg if w % 2 == 0 else h2p)[:, 0:4096]
                    def fn(t):
                        r = None
                        for j in range(4):
                            dc = dch + j
                            r = t.transpose(ps[:, j * 128:(j + 1) * 128],
                                            kcb2[:, a * D + dc * 128: a * D + (dc + 1) * 128],
                                            idf[:])
                        return r
                    return fn
                wl = [("SD", at(f"kc{w}"))]
                if tposes < 4:
                    wl.append(("VV", at("ppdone")))
                if tposes >= 2:
                    wl.append(("VV", mark[f"ktv{tposes-2}"]))
                emit("T", mkq(), waits=wl, inc="TT", tag=f"ktq{tposes}")
                def mkev2(w=w, a=a, dch=dch, q=q):
                    ps = pslist[q % 2]
                    stg2 = winp[:, (w % 2) * 4096:(w % 2) * 4096 + 4096]
                    def fn(v):
                        r = None
                        for j in range(4):
                            dc = dch + j
                            r = v.tensor_copy(stg2[:, dc * 512 + a * 128: dc * 512 + (a + 1) * 128],
                                              ps[:, j * 128:(j + 1) * 128])
                        return r
                    return fn
                emit("V", mkev2(), waits=[("TT", at(f"ktq{tposes}"))], inc="VV",
                     tag=f"ktv{tposes}")
                tposes += 1
            emit("T", lambda t: t.nop(), waits=[("VV", at(f"ktv{tposes-1}"))],
                 inc="TT", tag=f"ktt{w}")
            dma("Y", "SD", ktd[w], stg,
                waits=[("VV", at(f"ktv{tposes-1}"))], tag=f"ktw{w}")

        def emit_gsweeps():
          for sw in range(2):
              gslc = [(ps4[:, 0:512], ps4[:, 512:1024]),
                      (ps4[:, 1024:1536], ps4[:, 1536:2048]),
                      (ps1[:], ps2[:]), (ps5[:], ps6[:])]
              for w in range(NW):
                  waits = []
                  if sw == 0 and w < 2:
                      waits.append(("SD", at(f"ktw{NW-1}")))
                  if w >= 2 or sw > 0:
                      pw = (sw, w - 2) if w >= 2 else (sw - 1, NW - 2 + w)
                      waits.append(("TT", at(f"gmm{pw[0]}_{pw[1]}")))
                  kcb = vwin[:, (w % 2) * 4096:(w % 2) * 4096 + 4096]
                  dma("G", "GD", kcb.rearrange("p (a d) -> p a d", d=D),
                      k_d[w * 512:(w + 1) * 512, :].rearrange("(a p) d -> p a d", p=128),
                      waits=waits, tag=f"gw{sw}_{w}")

                  def mkg(sw=sw, w=w):
                      kcb2 = vwin[:, (w % 2) * 4096:(w % 2) * 4096 + 4096]
                      def fn(t):
                          r = None
                          for a in range(4):
                              for dtl in range(4):
                                  dt = sw * 4 + dtl
                                  for jw in range(2):
                                      r = t.matmul(
                                          gslc[dtl][jw],
                                          kcb2[:, a * D + dt * 128:a * D + (dt + 1) * 128],
                                          kcb2[:, a * D + jw * 512:a * D + (jw + 1) * 512],
                                          start=(w == 0 and a == 0),
                                          stop=(w == NW - 1 and a == 3))
                          return r
                      return fn
                  wl = [("GD", at(f"gw{sw}_{w}"))]
                  if sw == 0 and w == 0:
                      wl.append(("VV", at("modev1")))
                      wl.append(("VV", at("g0_lna_rv")))
                  if sw == 1 and w == 0:
                      wl.append(("VV", at("gev0")))
                  emit("T", mkg(), waits=wl, inc="TT", tag=f"gmm{sw}_{w}")
              def mkgev(sw=sw):
                  def fn(v):
                      r = None
                      for dtl in range(4):
                          for jw in range(2):
                              r = v.tensor_copy(hbuf[:, dtl * 1024 + jw * 512: dtl * 1024 + (jw + 1) * 512],
                                                gslc[dtl][jw])
                      return r
                  return fn
              gevw = [("TT", at(f"gmm{sw}_{NW-1}"))]
              if sw == 1:
                  gevw.append(("SD", at("gdw0")))
              emit("V", mkgev(), waits=gevw, inc="VV", tag=f"gev{sw}")
              dma("Y", "SD", gdr[sw * 512:(sw + 1) * 512, :].rearrange("(dt p) dd -> p dt dd", p=128),
                  hbuf[:].rearrange("p (dt dd) -> p dt dd", dd=1024),
                  waits=[("VV", at(f"gev{sw}"))], tag=f"gdw{sw}")

        GSEL = 16384

        def ppap(slot, dc):
            return sb_pp[:, dc, slot:slot + 1]

        for g in range(ng):
            gp = f"g{g}"
            def emit_xla(a, gg=None):
                gg = g if gg is None else gg
                wl = []
                if gg == 0 and a == 0:
                    wl.append(("TT", at("ppt7")))
                if a >= 2:
                    wl.append(("TT", at(f"g{gg}_xtq{(a-2)*2+1}")))
                if gg > 0 and a < 2:
                    wl.append(("TT", at(f"g{gg-1}_xtq{5 + 2*a}")))
                dma("Y", "SD", xtm[:, a % 2, :],
                    x_d[gg * TG + a * 128:gg * TG + (a + 1) * 128, :],
                    waits=wl, tag=f"g{gg}_xla{a}")
            if g == 0:
                emit_xla(0)
                emit_xla(1)
            for q in range(8):
                a, dch = q // 2, (q % 2) * 4
                if q in (2, 4) :
                    emit_xla(a + 1)
                pslist = [ps5, ps6, ps1, ps2]
                def mkxq(g=g, a=a, dch=dch, q=q):
                    ps = pslist[q % 4]
                    def fn(t):
                        r = None
                        for j in range(4):
                            dc = dch + j
                            r = t.transpose(ps[:, j * 128:(j + 1) * 128],
                                            xtm[:, a % 2, dc * 128:(dc + 1) * 128], idf[:])
                        return r
                    return fn
                wl = [("SD", at(f"{gp}_xla{a}"))]
                if g > 0 and q == 0:
                    wl.append(("VV", at(f"g{g-1}_otv7")))
                if g == 0 and q == 0:
                    wl.append(("VV", mark["ktv127"]))
                if q >= 4:
                    wl.append(("VV", at(f"{gp}_xev{q-4}")))
                emit("T", mkxq(), waits=wl, inc="TT", tag=f"{gp}_xtq{q}")
                def mkxev(g=g, a=a, dch=dch, q=q):
                    ps = pslist[q % 4]
                    def fn(v):
                        r = None
                        for j in range(4):
                            dc = dch + j
                            r = v.tensor_copy(xg[:, dc * TG + a * 128: dc * TG + (a + 1) * 128],
                                              ps[:, j * 128:(j + 1) * 128])
                        return r
                    return fn
                wl = [("TT", at(f"{gp}_xtq{q}"))]
                if g > 0 and q == 0:
                    wl.append(("GD", at(f"g{g-1}_ost")))
                emit("V", mkxev(), waits=wl, inc="VV", tag=f"{gp}_xev{q}")

            def XG(dc):
                return xg[:, dc * TG:(dc + 1) * TG]

            def HB(dc):
                return hbuf[:, dc * TG:(dc + 1) * TG]

            def XQ(dc):
                return xsq[:, dc * TG:(dc + 1) * TG]

            def H2(dc):
                return h2p[:, dc * TG:(dc + 1) * TG]

            def Y2(dc):
                return y2b[:, dc * TG:(dc + 1) * TG]

            def ln_stats(pref, src_ap_fn, src_is_bf, eps, wait_v_tag, extra_t_waits=(),
                         chunk_tags=None):
                def mksq():
                    def fn(s):
                        r = None
                        for dc in range(DC):
                            r = s.activation(XQ(dc), src_ap_fn(dc), AF.Square)
                        return r
                    return fn
                emit("S", mksq(), waits=[("VV", at(wait_v_tag))], inc="SS",
                     tag=f"{pref}_sq")
                for dc in range(DC):
                    def mksx(dc=dc):
                        def fn(t):
                            return t.matmul(ps1[0:1, :], onb[:] if src_is_bf else onf[:],
                                            src_ap_fn(dc), start=(dc == 0),
                                            stop=(dc == DC - 1))
                        return fn
                    wl = list(extra_t_waits) if dc == 0 else []
                    if chunk_tags is not None:
                        wl.append(("VV", at(chunk_tags[dc])))
                    elif dc == 0:
                        wl.append(("VV", at(wait_v_tag)))
                    emit("T", mksx(dc), waits=wl, inc="TT", tag=f"{pref}_sx{dc}")
                def mkst():
                    def fn(t):
                        r = None
                        for dc in range(DC):
                            r = t.matmul(ps2[0:1, :], onb[:], XQ(dc),
                                         start=(dc == 0), stop=(dc == DC - 1))
                        return r
                    return fn
                emit("T", mkst(), waits=[("SS", at(f"{pref}_sq"))], inc="TT",
                     tag=f"{pref}_st")
                def mkv():
                    def fn(v):
                        v.tensor_copy(rowb[0:1, MU:MU + TG], ps1[0:1, :])
                        v.tensor_copy(rowb[0:1, E2:E2 + TG], ps2[0:1, :])
                        v.tensor_tensor(rowb[0:1, TMP:TMP + TG], rowb[0:1, MU:MU + TG],
                                        rowb[0:1, MU:MU + TG], ALU.mult)
                        v.tensor_tensor(rowb[0:1, E2:E2 + TG], rowb[0:1, E2:E2 + TG],
                                        rowb[0:1, TMP:TMP + TG], ALU.subtract)
                        return v.tensor_scalar_add(rowb[0:1, E2:E2 + TG],
                                                   rowb[0:1, E2:E2 + TG], eps)
                    return fn
                vwl = [("TT", at(f"{pref}_st"))]
                if pref == "g0_lna":
                    vwl.append(("GD", at("ppw2")))
                emit("V", mkv(), waits=vwl, inc="VV", tag=f"{pref}_v1")
                emit("S", lambda s: s.activation(rowb[0:1, TMP:TMP + TG],
                                                 rowb[0:1, E2:E2 + TG], AF.Sqrt),
                     waits=[("VV", at(f"{pref}_v1"))], inc="SS", tag=f"{pref}_s1")
                emit("V", lambda v: v.reciprocal(rowb[0:1, RR:RR + TG],
                                                 rowb[0:1, TMP:TMP + TG]),
                     waits=[("SS", at(f"{pref}_s1"))], inc="VV", tag=f"{pref}_v2")
                def mkrep():
                    def fn(t):
                        t.matmul(ps4[:, 0:512], o1r[:], rowb[0:1, MU:MU + TG],
                                 start=True, stop=True)
                        return t.matmul(ps4[:, 512:1024], o1r[:], rowb[0:1, RR:RR + TG],
                                        start=True, stop=True)
                    return fn
                emit("T", mkrep(), waits=[("VV", at(f"{pref}_v2"))], inc="TT",
                     tag=f"{pref}_rp")
                def mkrev():
                    def fn(v):
                        v.tensor_copy(reps[:, 0, :], ps4[:, 0:512])
                        return v.tensor_copy(reps[:, 1, :], ps4[:, 512:1024])
                    return fn
                emit("V", mkrev(), waits=[("TT", at(f"{pref}_rp"))], inc="VV",
                     tag=f"{pref}_rv")

            ln_stats(f"{gp}_lna", XG, False, 1e-5, f"{gp}_xev7",
                     extra_t_waits=([("VV", at(f"g{g-1}_b7d"))] if g > 0 else
                                    [("VV", at("ppdone"))]))

            if g == 0:
                emit_gsweeps()
                emit_staging_casts()
            for dc in range(DC):
                def mkh(g=g, dc=dc):
                    def fn(v):
                        v.tensor_tensor(scr[:], XG(dc), reps[:, 0, :], ALU.subtract)
                        v.tensor_tensor(scr[:], scr[:], reps[:, 1, :], ALU.mult)
                        return v.tensor_scalar(HB(dc), scr[:], ppap(PP_CA1, dc),
                                               ppap(PP_SA, dc), ALU.mult, ALU.add)
                    return fn
                hwl = []
                if dc == 0:
                    hwl = [("VV", at(f"{gp}_lna_rv")), ("VV", at("ppdone"))]
                    if g == 0:
                        hwl.append(("SD", at("gdw1")))
                emit("V", mkh(g, dc), waits=hwl, inc="VV", tag=f"{gp}_h{dc}")
            mark[f"{gp}_h"] = mark[f"{gp}_h{DC-1}"]

            gtgt = [ps4[:, 0:512], ps4[:, 512:1024], ps4[:, 1024:1536],
                    ps4[:, 1536:2048], ps1[:], ps2[:], ps5[:], ps6[:]]
            wl = [("SD", at("gdw1"))]
            if g > 0:
                wl.append(("TT", at(f"g{g-1}_at{NW-1}")))
            dma("Y", "SD", vwin[:].rearrange("p (c n) -> p c n", n=1024),
                gdr[:, :].rearrange("(c p) n -> p c n", p=128),
                waits=wl, tag=f"{gp}_gw")
            for dc in range(DC):
                def mkgh(g=g, dc=dc):
                    def fn(t):
                        r = None
                        for dt in range(DC):
                            r = t.matmul(gtgt[dt],
                                         vwin[:, dc * 1024 + dt * 128: dc * 1024 + (dt + 1) * 128],
                                         HB(dc), start=(dc == 0), stop=(dc == DC - 1))
                        return r
                    return fn
                wl = [("VV", at(f"{gp}_h{dc}"))]
                if dc == 0:
                    wl.append(("SD", at(f"{gp}_gw")))
                emit("T", mkgh(g, dc), waits=wl, inc="TT", tag=f"{gp}_ghc{dc}")
            mark[f"{gp}_gh"] = mark[f"{gp}_ghc{DC-1}"]
            def mkpm(g=g):
                def fn(v):
                    r = None
                    for dt in range(DC):
                        r = v.tensor_mul(XQ(dt), gtgt[dt], HB(dt))
                    return r
                return fn
            emit("V", mkpm(g), waits=[("TT", at(f"{gp}_gh"))], inc="VV",
                 tag=f"{gp}_pm1")
            if dbg and g == 0:
                dma("G", "GD", dbg_t["dbg_P"], xsq[:], waits=[("VV", at(f"{gp}_pm1"))],
                    tag="tap_P")
                dma("G", "GD", dbg_t["dbg_hh"], hbuf[:],
                    waits=[("VV", at(f"{gp}_h"))], tag="tap_hh")
            def mkn2(g=g):
                def fn(t):
                    r = None
                    for dc in range(DC):
                        r = t.matmul(ps1[0:1, :], on1b[:], XQ(dc),
                                     start=(dc == 0), stop=(dc == DC - 1))
                    return r
                return fn
            emit("T", mkn2(g), waits=[("VV", at(f"{gp}_pm1"))], inc="TT", tag=f"{gp}_n2")
            def mkn2v(g=g):
                def fn(v):
                    return v.reciprocal(rowb[0:1, TMP:TMP + TG], ps1[0:1, :])
                return fn
            emit("V", mkn2v(g), waits=[("TT", at(f"{gp}_n2"))], inc="VV", tag=f"{gp}_n2v")
            emit("S", lambda s: s.activation(rowb[0:1, N2:N2 + TG],
                                             rowb[0:1, TMP:TMP + TG], AF.Sqrt,
                                             scale=float(NBANK)),
                 waits=[("VV", at(f"{gp}_n2v"))], inc="SS", tag=f"{gp}_srt")
            emit("T", lambda t, g=g: t.matmul(ps2[:, :], o1r[:], rowb[0:1, N2:N2 + TG],
                                              start=True, stop=True),
                 waits=[("SS", at(f"{gp}_srt"))], inc="TT", tag=f"{gp}_srp")
            emit("V", lambda v, g=g: v.tensor_copy(reps[:, 0, :], ps2[:, :]),
                 waits=[("TT", at(f"{gp}_srp"))], inc="VV", tag=f"{gp}_sev")
            def mkhs(g=g):
                def fn(v):
                    r = None
                    for dc in range(DC):
                        r = v.tensor_tensor(HB(dc), HB(dc), reps[:, 0, :], ALU.mult)
                    return r
                return fn
            hswl = [("VV", at(f"{gp}_sev"))]
            if dbg and g == 0:
                hswl.append(("GD", at("tap_hh")))
                hswl.append(("GD", at("tap_P")))
            emit("V", mkhs(g), waits=hswl, inc="VV", tag=f"{gp}_hs")

            pslist = [ps1, ps2, ps5, ps6]
            for w in range(NW):
                ktb = winp[:, (w % 2) * 4096:(w % 2) * 4096 + 4096]
                wl = []
                if g == 0 and w < 2:
                    wl.append(("SD", at(f"ktw{NW-1}")))
                if g > 0 and w < 2:
                    wl.append(("TT", at(f"g{g-1}_wo15")))
                if w >= 2:
                    wl.append(("TT", at(f"{gp}_sc{(w-2)*4+3}")))
                dma("Y", "SD", ktb, ktd[w], waits=wl, tag=f"{gp}_ktw{w}")
                for ntl in range(4):
                    nt = w * 4 + ntl
                    def mksc(g=g, w=w, ntl=ntl, nt=nt):
                        ps = pslist[nt % 4]
                        ktb2 = winp[:, (w % 2) * 4096:(w % 2) * 4096 + 4096]
                        def fn(t):
                            r = None
                            for dc in range(DC):
                                r = t.matmul(ps[:],
                                             ktb2[:, dc * 512 + ntl * 128: dc * 512 + (ntl + 1) * 128],
                                             HB(dc), start=(dc == 0), stop=(dc == DC - 1))
                            return r
                        return fn
                    wl = [("SD", at(f"{gp}_ktw{w}")), ("VV", at(f"{gp}_hs"))]
                    if nt >= 4:
                        wl.append(("SS", at(f"{gp}_ge{nt-4}")))
                    elif nt == 0:
                        wl.append(("SS", at(f"{gp}_srt")))
                        wl.append(("VV", at(f"{gp}_sev")))
                    emit("T", mksc(g, w, ntl, nt), waits=wl, inc="TT", tag=f"{gp}_sc{nt}")
                    def mkge(g=g, nt=nt):
                        ps = pslist[nt % 4]
                        def fn(s):
                            return s.activation(big[:, nt * 512:(nt + 1) * 512], ps[:],
                                                AF.Gelu)
                        return fn
                    emit("S", mkge(g, nt), waits=[("TT", at(f"{gp}_sc{nt}"))],
                         inc="SS", tag=f"{gp}_ge{nt}")

            if dbg and g == 0:
                dma("G", "GD", dbg_t["dbg_gated"], big[:, 0:32768],
                    waits=[("SS", at(f"{gp}_ge{NCH-1}"))], tag="tap_gated")

            atgt = [ps4[:, 0:512], ps4[:, 512:1024], ps4[:, 1024:1536], ps4[:, 1536:2048],
                    ps1[:], ps2[:], ps5[:], ps6[:]]
            for w in range(NW):
                vb = vwin[:, (w % 2) * 4096:(w % 2) * 4096 + 4096]
                wl = [("GD", at("vbf3"))]
                if w < 2:
                    wl.append(("TT", at(f"{gp}_gh")))
                else:
                    wl.append(("TT", at(f"{gp}_at{w-2}")))
                dma("Y", "SD", vb.rearrange("p (a d) -> p a d", d=D),
                    vbf[w * 512:(w + 1) * 512, :].rearrange("(a p) d -> p a d", p=128),
                    waits=wl, tag=f"{gp}_vw{w}")
                def mkat(g=g, w=w):
                    vb2 = vwin[:, (w % 2) * 4096:(w % 2) * 4096 + 4096]
                    def fn(t):
                        r = None
                        for a in range(4):
                            nch = w * 4 + a
                            for dt in range(DC):
                                r = t.matmul(atgt[dt],
                                             vb2[:, a * D + dt * 128: a * D + (dt + 1) * 128],
                                             big[:, nch * 512:(nch + 1) * 512],
                                             start=(nch == 0), stop=(nch == NCH - 1))
                        return r
                    return fn
                wl = [("SD", at(f"{gp}_vw{w}"))]
                if w == 0:
                    wl.append(("SS", at(f"{gp}_ge{NCH-1}")))
                emit("T", mkat(g, w), waits=wl, inc="TT", tag=f"{gp}_at{w}")

            if g + 1 < ng:
                emit_xla(0, g + 1)
                emit_xla(1, g + 1)

            for dt in range(DC):
                def mka8(g=g, dt=dt):
                    def fn(v):
                        return v.scalar_tensor_tensor(XG(dt), atgt[dt], ppap(PP_GA, dt),
                                                      XG(dt), ALU.mult, ALU.add)
                    return fn
                wl = [("TT", at(f"{gp}_at{NW-1}"))] if dt == 0 else []
                emit("V", mka8(g, dt), waits=wl, inc="VV", tag=f"{gp}_a8_{dt}")
            mark[f"{gp}_a8"] = mark[f"{gp}_a8_{DC-1}"]
            if dbg and g == 0:
                dma("G", "GD", dbg_t["dbg_x2"], xg[:], waits=[("VV", at(f"{gp}_a8"))],
                    tag="tap_x2")

            ln_stats(f"{gp}_lnb", XG, False, 1e-5, f"{gp}_a8",
                     extra_t_waits=[],
                     chunk_tags=[f"{gp}_a8_{max(dc, 4)}" for dc in range(DC)])
            for dc in range(DC):
                def mkh2(g=g, dc=dc):
                    def fn(v):
                        v.tensor_tensor(scr[:], XG(dc), reps[:, 0, :], ALU.subtract)
                        v.tensor_tensor(scr[:], scr[:], reps[:, 1, :], ALU.mult)
                        return v.tensor_scalar(H2(dc), scr[:], ppap(PP_CB1, dc),
                                               ppap(PP_SB, dc), ALU.mult, ALU.add)
                    return fn
                wl = []
                if dc == 0:
                    wl = [("VV", at(f"{gp}_lnb_rv"))]
                    if g > 0:
                        wl.append(("VV", at(f"g{g-1}_otv7")))
                emit("V", mkh2(g, dc), waits=wl, inc="VV", tag=f"{gp}_h2_{dc}")
            mark[f"{gp}_h2"] = mark[f"{gp}_h2_{DC-1}"]
            if dbg and g == 0:
                dma("G", "GD", dbg_t["dbg_h2"], h2p[:], waits=[("VV", at(f"{gp}_h2"))],
                    tag="tap_h2")

            rtgt = [ps5[0:1, :], ps6[0:1, :], ps1[0:1, :], ps2[0:1, :]]
            for dc in range(DC):
                def mkrt(g=g, dc=dc):
                    def fn(t):
                        r = None
                        for e in range(E):
                            r = t.matmul(rtgt[e], sb_wr[:, dc, e:e + 1], H2(dc),
                                         start=(dc == 0), stop=(dc == DC - 1))
                        return r
                    return fn
                wl = [("VV", at(f"{gp}_h2_{dc}"))]
                if dc == 0:
                    wl.append(("GD", at("wr_ld")))
                emit("T", mkrt(g, dc), waits=wl, inc="TT", tag=f"{gp}_rtc{dc}")
            mark[f"{gp}_rt"] = mark[f"{gp}_rtc{DC-1}"]

            def mktn(g=g):
                def fn(v):
                    for e in range(E):
                        v.tensor_copy(rowb[0:1, LCAT + e * TG:LCAT + (e + 1) * TG],
                                      rtgt[e])
                    L = lambda e: rowb[0:1, LCAT + e * TG:LCAT + (e + 1) * TG]
                    M = lambda e: rowb[0:1, MASK + e * TG:MASK + (e + 1) * TG]
                    tmp = rowb[0:1, TMP:TMP + TG]
                    mx = rowb[0:1, MU:MU + TG]
                    v.tensor_tensor(tmp, L(0), L(1), ALU.max)
                    v.tensor_tensor(mx, L(2), L(3), ALU.max)
                    v.tensor_tensor(mx, tmp, mx, ALU.max)
                    v.tensor_tensor(M(0), L(0), mx, ALU.is_ge)
                    rem = rowb[0:1, RR:RR + TG]
                    v.tensor_scalar(rem, M(0), -1.0, 1.0, ALU.mult, ALU.add)
                    v.tensor_tensor(M(1), L(1), mx, ALU.is_ge)
                    v.tensor_tensor(M(1), M(1), rem, ALU.mult)
                    v.tensor_scalar(tmp, M(1), -1.0, 1.0, ALU.mult, ALU.add)
                    v.tensor_tensor(rem, rem, tmp, ALU.mult)
                    v.tensor_tensor(M(2), L(2), mx, ALU.is_ge)
                    v.tensor_tensor(M(2), M(2), rem, ALU.mult)
                    v.tensor_scalar(tmp, M(2), -1.0, 1.0, ALU.mult, ALU.add)
                    v.tensor_tensor(rem, rem, tmp, ALU.mult)
                    return v.tensor_copy(M(3), rem)
                return fn
            emit("V", mktn(g), waits=[("TT", at(f"{gp}_rt"))], inc="VV", tag=f"{gp}_tn")
            dma("Y", "SD", msc2.rearrange("(one t) -> one t", one=1),
                rowb[0:1, MASK:MASK + E * TG],
                waits=[("VV", at(f"{gp}_tn"))], tag=f"{gp}_m40")
            dma("Y", "SD", sb_m4[:], msc2.rearrange("(e t) -> e t", e=E),
                waits=[("SD", at(f"{gp}_m40"))], tag=f"{gp}_m4")
            emit("V", lambda v, g=g: v.tensor_copy(sb_m4b[:], sb_m4[:]),
                 waits=[("SD", at(f"{gp}_m4"))], inc="VV", tag=f"{gp}_m4b")

            mtgt = [ps6[:], ps5[:], ps1[:], ps2[:]]
            def mkmr(g=g):
                def fn(t):
                    r = None
                    for e in range(E):
                        r = t.matmul(mtgt[e], o1r[:],
                                     rowb[0:1, MASK + e * TG:MASK + (e + 1) * TG],
                                     start=True, stop=True)
                    return r
                return fn
            emit("T", mkmr(g), waits=[("VV", at(f"{gp}_tn")), ("VV", at(f"{gp}_a8"))],
                 inc="TT", tag=f"{gp}_mrt")
            for e in range(E):
                def mkh2m(g=g, e=e):
                    def fn(v):
                        r = None
                        for dc in range(DC):
                            r = v.tensor_mul(big[:, (e * DC + dc) * 512:(e * DC + dc + 1) * 512],
                                             H2(dc), mtgt[e])
                        return r
                    return fn
                wl = [("TT", at(f"{gp}_mrt"))] if e == 0 else []
                if dbg and g == 0 and e == 0:
                    wl.append(("GD", at("tap_gated")))
                emit("V", mkh2m(g, e), waits=wl, inc="VV", tag=f"{gp}_mr{e}")

            pslist2 = [ps6, ps5, ps1, ps2]

            def emit_w1w(it, half):
                w1b = winp[:, (it % 2) * 4096:(it % 2) * 4096 + 4096]
                wl = [("GD", at("w1bf3_1"))]
                if it < 2:
                    wl.append(("TT", at(f"{gp}_sc{NCH-1}")))
                else:
                    wl.append(("TT", at(f"{gp}_he{it-2}")))
                es = (0, 1) if half == 0 else (2, 3)
                qeng, qsem = ("Y", "SD") if half == 0 else ("S", "SC")
                for e in es:
                    dma(qeng, qsem,
                        w1b[:, e * 1024:(e + 1) * 1024].rearrange("p (c i) -> p c i", i=128),
                        w1bf[e, :, it * 128:(it + 1) * 128].rearrange("(c p) i -> p c i", p=128),
                        waits=(wl if e in es[:1] else []), tag=f"{gp}_w1w{it}_{e}")
            emit_w1w(0, 1)
            emit_w1w(1, 1)
            for it in range(IT):
                emit_w1w(it, 0)
                for e in range(E):
                    def mkhee(g=g, it=it, e=e):
                        ps = pslist2[it % 4]
                        w1b2 = winp[:, (it % 2) * 4096:(it % 2) * 4096 + 4096]
                        def fn(t):
                            r = None
                            for dc in range(DC):
                                r = t.matmul(ps[:],
                                             w1b2[:, (e * DC + dc) * 128:(e * DC + dc + 1) * 128],
                                             big[:, (e * DC + dc) * 512:(e * DC + dc + 1) * 512],
                                             start=(e == 0 and dc == 0), stop=False)
                            return r
                        return fn
                    wl2 = [("VV", at(f"{gp}_mr{e}"))]
                    if e == 0 and it < 4:
                        wl2 = [("VV", at(f"{gp}_mr{it}"))]
                    if e < 2:
                        wl2.append(("SD", at(f"{gp}_w1w{it}_{e}")))
                    else:
                        wl2.append(("SC", at(f"{gp}_w1w{it}_{e}")))
                    if e == 0 and it >= 4:
                        wl2.append(("SS", at(f"{gp}_fg{it-4}")))
                    emit("T", mkhee(g, it, e), waits=wl2, inc="TT", tag=f"{gp}_hee{it}_{e}")
                def mkheb(g=g, it=it):
                    ps = pslist2[it % 4]
                    def fn(t):
                        return t.matmul(ps[:], sb_b1[:, it * 128:(it + 1) * 128],
                                        sb_m4b[:], start=False, stop=True)
                    return fn
                emit("T", mkheb(g, it), waits=[("VV", at(f"{gp}_m4b")), ("GD", at("b1_ld"))],
                     inc="TT", tag=f"{gp}_he{it}")
                if it + 2 < IT:
                    emit_w1w(it + 2, 1)
                def mkfg(g=g, it=it):
                    ps = pslist2[it % 4]
                    def fn(s):
                        return s.activation(big[:, GSEL + it * 512:GSEL + (it + 1) * 512],
                                            ps[:], AF.Gelu)
                    return fn
                fgw = [("TT", at(f"{gp}_he{it}"))]
                if dbg and g == 0 and it == 0:
                    fgw.append(("GD", at("tap_gated")))
                emit("S", mkfg(g, it), waits=fgw, inc="SS", tag=f"{gp}_fg{it}")

            for hh in range(2):
                for ic4 in range(8):
                    wi = hh * 8 + ic4
                    wob = winp[:, (wi % 2) * 2048 + 4096 * 0:(wi % 2) * 2048 + 2048]
                    wl = []
                    if wi < 2:
                        wl.append(("TT", at(f"{gp}_he{IT-2}")))
                        wl.append(("GD", at("wobf1")))
                    else:
                        wl.append(("TT", at(f"{gp}_wo{wi-2}")))
                    dma("Y", "SD", wob.rearrange("p (a h) -> p a h", h=512),
                        wobf[ic4 * 512:(ic4 + 1) * 512, hh * 512:(hh + 1) * 512]
                        .rearrange("(a p) h -> p a h", p=128),
                        waits=wl, tag=f"{gp}_wow{wi}")
                    def mkwo(g=g, hh=hh, ic4=ic4, wi=wi):
                        wob2 = winp[:, (wi % 2) * 2048:(wi % 2) * 2048 + 2048]
                        def fn(t):
                            r = None
                            for a in range(4):
                                ic = ic4 * 4 + a
                                for htl in range(4):
                                    r = t.matmul(ps4[:, htl * 512:(htl + 1) * 512],
                                                 wob2[:, a * 512 + htl * 128: a * 512 + (htl + 1) * 128],
                                                 big[:, GSEL + ic * 512:GSEL + (ic + 1) * 512],
                                                 start=(ic == 0), stop=(ic == IT - 1))
                            return r
                        return fn
                    wl2 = [("SD", at(f"{gp}_wow{wi}"))]
                    if wi == 0:
                        wl2.append(("SS", at(f"{gp}_fg{IT-1}")))
                        wl2.append(("VV", at(f"{gp}_lnb_rv")))
                    if ic4 == 0 and hh == 1:
                        wl2.append(("SS", at(f"{gp}_ye0")))
                    emit("T", mkwo(g, hh, ic4, wi), waits=wl2, inc="TT",
                         tag=f"{gp}_wo{wi}")
                def mkye(g=g, hh=hh):
                    def fn(s):
                        r = None
                        for htl in range(4):
                            dt = hh * 4 + htl
                            r = s.activation(Y2(dt), ps4[:, htl * 512:(htl + 1) * 512],
                                             AF.Identity, bias=ppap(PP_BO, dt))
                        return r
                    return fn
                emit("S", mkye(g, hh), waits=[("TT", at(f"{gp}_wo{hh*8+7}"))],
                     inc="SS", tag=f"{gp}_ye{hh}")
                def mkya(g=g, hh=hh):
                    def fn(v):
                        r = None
                        for htl in range(4):
                            dt = hh * 4 + htl
                            r = v.tensor_tensor(Y2(dt), Y2(dt), H2(dt), ALU.add)
                        return r
                    return fn
                emit("V", mkya(g, hh), waits=[("SS", at(f"{gp}_ye{hh}"))], inc="VV",
                     tag=f"{gp}_ya{hh}")

            ln_stats(f"{gp}_lnc", Y2, True, 1e-12, f"{gp}_ya1",
                     extra_t_waits=[],
                     chunk_tags=[f"{gp}_ya{0 if dc < 4 else 1}" for dc in range(DC)])

            if dbg and g == 0:
                pass
            for dc in range(DC):
                def mkb7(g=g, dc=dc):
                    def fn(v):
                        v.tensor_tensor(scr[:], Y2(dc), reps[:, 0, :], ALU.subtract)
                        v.tensor_tensor(scr[:], scr[:], reps[:, 1, :], ALU.mult)
                        v.tensor_scalar(scr[:], scr[:], ppap(PP_GGB, dc),
                                        ppap(PP_BGB, dc), ALU.mult, ALU.add)
                        return v.tensor_tensor(H2(dc), scr[:], XG(dc), ALU.add)
                    return fn
                b7w = []
                if dc == 0:
                    b7w = [("VV", at(f"{gp}_lnc_rv"))]
                    if dbg and g == 0:
                        b7w.append(("GD", at("tap_h2")))
                emit("V", mkb7(g, dc), waits=b7w, inc="VV", tag=f"{gp}_b7_{dc}")
            mark[f"{gp}_b7d"] = mark[f"{gp}_b7_{DC-1}"]

            pslist3 = [ps5, ps6, ps1, ps2]
            for q in range(8):
                a, dch = q // 2, (q % 2) * 4
                def mkot(g=g, a=a, dch=dch, q=q):
                    ps = pslist3[q % 4]
                    def fn(t):
                        r = None
                        for j in range(4):
                            dc = dch + j
                            r = t.transpose(ps[:, j * 128:(j + 1) * 128],
                                            h2p[:, dc * TG + a * 128: dc * TG + (a + 1) * 128],
                                            idf[:])
                        return r
                    return fn
                wl = [("VV", at(f"{gp}_b7_{3 if (q % 2) == 0 else 7}"))]
                if q >= 4:
                    wl.append(("VV", at(f"{gp}_otv{q-4}")))
                emit("T", mkot(g, a, dch, q), waits=wl, inc="TT", tag=f"{gp}_ot{q}")
                def mkotv(g=g, a=a, dch=dch, q=q):
                    ps = pslist3[q % 4]
                    def fn(v):
                        r = None
                        for j in range(4):
                            dc = dch + j
                            r = v.tensor_copy(xg[:, a * D + dc * 128: a * D + (dc + 1) * 128],
                                              ps[:, j * 128:(j + 1) * 128])
                        return r
                    return fn
                owl = [("TT", at(f"{gp}_ot{q}"))]
                if dbg and g == 0 and q == 0:
                    owl.append(("GD", at("tap_x2")))
                emit("V", mkotv(g, a, dch, q), waits=owl, inc="VV", tag=f"{gp}_otv{q}")
            dma("G", "GD", out_d[g * TG:(g + 1) * TG, :].rearrange("(a p) d -> p a d", p=128),
                xg[:].rearrange("p (a d) -> p a d", d=D),
                waits=[("VV", at(f"{gp}_otv7"))], tag=f"{gp}_ost")

        if dbg:
            barrier = [(sm, cnt[sm]) for sm in ("TT", "VV", "SS")]
            dma("G", "GD", dbg_t["dbg_h"], hbuf[:], waits=barrier)
            dma("G", "GD", dbg_t["dbg_y2"], y2b[:], waits=barrier)
            dma("G", "GD", dbg_t["dbg_gsel"], big[:, GSEL:GSEL + 16384], waits=barrier)
            dma("G", "GD", dbg_t["dbg_h2m"], big[:, 0:16384], waits=barrier)
            dma("G", "GD", dbg_t["dbg_rowb"], rowb[:], waits=barrier)
            dma("G", "GD", dbg_t["dbg_pp"], sb_pp[:], waits=barrier)
            dma("G", "GD", dbg_t["dbg_c"], sb_c[:], waits=barrier)
            dma("G", "GD", dbg_t["dbg_gdr"], gdr, waits=barrier)
            dma("G", "GD", dbg_t["dbg_ktd"], ktd, waits=barrier)

        final_gd = cnt["GD"]
        final_sd = cnt["SD"]
        emit("G", lambda g: g.wait_ge(sem["GD"], final_gd), inc=None)
        emit("Y", lambda y: y.wait_ge(sem["SD"], final_sd), inc=None)

        block = en(nc.Block())

        def replay(EE, key):
            lastw = {}
            for w, fn, inc, n in ops[key]:
                for s, v in w:
                    if lastw.get(s, -1) < v:
                        EE.wait_ge(sem[s], v)
                        lastw[s] = v
                r = fn(EE)
                if inc is not None:
                    if r is None:
                        EE.sem_inc(sem[inc], n)
                    else:
                        r.then_inc(sem[inc], n)

        @block.tensor
        def _(t):
            replay(t, "T")

        @block.vector
        def _(v):
            replay(v, "V")

        @block.scalar
        def _(s):
            replay(s, "S")

        @block.sync
        def _(y):
            replay(y, "Y")

        @block.gpsimd
        def _(gg):
            replay(gg, "G")

    return nc


_NC_CACHE = None


def kernel(L0_attn_q, L0_attn_k, L0_attn_v, L0_ffn_W_router, L0_ffn_W1, L0_ffn_b1,
           L0_ffn_Wo, L0_ffn_bo, L0_ffn_ln_gamma, L0_ffn_ln_beta, L0_c, L0_wmod):
    global _NC_CACHE
    if _NC_CACHE is None:
        _NC_CACHE = build_nc()
    nc = _NC_CACHE
    x = np.ascontiguousarray(np.asarray(L0_attn_q, dtype=np.float32))
    com = {
        "k": np.ascontiguousarray(np.asarray(L0_attn_k, np.float32)),
        "v": np.ascontiguousarray(np.asarray(L0_attn_v, np.float32)),
        "wr": np.ascontiguousarray(np.asarray(L0_ffn_W_router, np.float32)),
        "w1": np.ascontiguousarray(np.asarray(L0_ffn_W1, np.float32)),
        "b1": np.ascontiguousarray(np.asarray(L0_ffn_b1, np.float32)),
        "wo": np.ascontiguousarray(np.asarray(L0_ffn_Wo, np.float32)),
        "bo": np.ascontiguousarray(np.asarray(L0_ffn_bo, np.float32)),
        "gam": np.ascontiguousarray(np.asarray(L0_ffn_ln_gamma, np.float32)),
        "bet": np.ascontiguousarray(np.asarray(L0_ffn_ln_beta, np.float32)),
        "wmod": np.ascontiguousarray(np.asarray(L0_wmod, np.float32)),
    }
    c = np.ascontiguousarray(np.asarray(L0_c, np.float32))
    in_maps = []
    for i in range(8):
        m = dict(com)
        m["x"] = np.ascontiguousarray(x[i])
        m["c"] = np.ascontiguousarray(c[i:i + 1])
        in_maps.append(m)
    res = run_bass_kernel_spmd(nc, in_maps, core_ids=list(range(8)))
    return np.stack([r["out"] for r in res.results], axis=0)


# version 18
# speedup vs baseline: 1.0442x; 1.0130x over previous
import numpy as np
import concourse.bass as bass
import concourse.mybir as mybir
from concourse.bass_utils import run_bass_kernel_spmd

F32 = mybir.dt.float32
BF16 = mybir.dt.bfloat16
AF = mybir.ActivationFunctionType
ALU = mybir.AluOpType

T = 4096
D = 1024
DC = 8
TG = 512
NG = T // TG
NBANK = 8192
NCH = NBANK // 128
NW = 16
I_SEL = 4096
IT = 32
E = 4
SQRT_N = float(np.sqrt(np.float32(NBANK)))

MU, RR, E2, N2, TMP = 0, 512, 1024, 1536, 2048
LCAT, MASK = 2560, 4608
ROWLEN = 6656
MODR = 0


def build_nc(ng=NG, dbg=False):
    nc = bass.Bass("TRN2", target_bir_lowering=False)

    x_d = nc.dram_tensor("x", [T, D], F32, kind="ExternalInput").ap()
    c_d = nc.dram_tensor("c", [1, D], F32, kind="ExternalInput").ap()
    k_d = nc.dram_tensor("k", [NBANK, D], F32, kind="ExternalInput").ap()
    v_d = nc.dram_tensor("v", [NBANK, D], F32, kind="ExternalInput").ap()
    wr_d = nc.dram_tensor("wr", [D, E], F32, kind="ExternalInput").ap()
    w1_d = nc.dram_tensor("w1", [E, D, I_SEL], F32, kind="ExternalInput").ap()
    b1_d = nc.dram_tensor("b1", [E, I_SEL], F32, kind="ExternalInput").ap()
    wo_d = nc.dram_tensor("wo", [I_SEL, D], F32, kind="ExternalInput").ap()
    bo_d = nc.dram_tensor("bo", [D], F32, kind="ExternalInput").ap()
    gam_d = nc.dram_tensor("gam", [D], F32, kind="ExternalInput").ap()
    bet_d = nc.dram_tensor("bet", [D], F32, kind="ExternalInput").ap()
    wmod_d = nc.dram_tensor("wmod", [D, 6 * D], F32, kind="ExternalInput").ap()
    out_d = nc.dram_tensor("out", [T, D], F32, kind="ExternalOutput").ap()

    vbf = nc.dram_tensor("vbf", [NBANK, D], BF16).ap()
    w1bf = nc.dram_tensor("w1bf", [E, D, I_SEL], BF16).ap()
    wobf = nc.dram_tensor("wobf", [I_SEL, D], BF16).ap()
    ktd = nc.dram_tensor("ktd", [NW, 128, DC * 512], BF16).ap()
    gdr = nc.dram_tensor("gdr", [D, D], BF16).ap()
    mkind = dict(kind="ExternalOutput") if dbg else {}
    msc1 = nc.dram_tensor("msc1", [E * TG], F32, **mkind).ap()
    msc2 = nc.dram_tensor("msc2", [E * TG], F32, **mkind).ap()
    if dbg:
        dbg_t = {}
        for nm, shp, dt in [("dbg_h", [128, 4096], F32), ("dbg_x2", [128, 4096], F32),
                            ("dbg_h2", [128, 4096], F32), ("dbg_gated", [128, 32768], F32),
                            ("dbg_y2", [128, 4096], F32), ("dbg_gsel", [128, 16384], F32),
                            ("dbg_h2m", [128, 16384], F32), ("dbg_rowb", [1, ROWLEN], F32),
                            ("dbg_pp", [128, DC, 16], F32), ("dbg_c", [128, DC], F32),
                            ("dbg_gdr", [D, D], F32), ("dbg_ktd", [NW, 128, DC * 512], F32),
                            ("dbg_P", [128, 4096], F32), ("dbg_hh", [128, 4096], F32)]:
            dbg_t[nm] = nc.dram_tensor(nm, shp, dt, kind="ExternalOutput").ap()

    from contextlib import ExitStack
    with ExitStack() as st:
        en = st.enter_context
        sb_pp = en(nc.sbuf_tensor("sb_pp", [128, DC, 16], F32))
        sb_c = en(nc.sbuf_tensor("sb_c", [128, DC], F32))
        sb_wr = en(nc.sbuf_tensor("sb_wr", [128, DC, E], F32))
        sb_b1 = en(nc.sbuf_tensor("sb_b1", [E, I_SEL], BF16))
        sb_m4 = en(nc.sbuf_tensor("sb_m4", [E, TG], F32))
        sb_m4b = en(nc.sbuf_tensor("sb_m4b", [E, TG], BF16))
        sb_l4 = en(nc.sbuf_tensor("sb_l4", [E, TG], F32))
        rowb = en(nc.sbuf_tensor("rowb", [1, ROWLEN], F32))
        idf = en(nc.sbuf_tensor("idf", [128, 128], F32))
        idb = en(nc.sbuf_tensor("idb", [128, 128], BF16))
        iot = en(nc.sbuf_tensor("iot", [128, 128], F32))
        onf = en(nc.sbuf_tensor("onf", [128, 1], F32))
        onb = en(nc.sbuf_tensor("onb", [128, 1], BF16))
        on1b = en(nc.sbuf_tensor("on1b", [128, 1], BF16))
        o1r = en(nc.sbuf_tensor("o1r", [1, 128], F32))
        xtm = en(nc.sbuf_tensor("xtm", [128, 2, 1024], F32))
        xg = en(nc.sbuf_tensor("xg", [128, DC * TG], F32))
        hbuf = en(nc.sbuf_tensor("hbuf", [128, DC * TG], BF16))
        xsq = en(nc.sbuf_tensor("xsq", [128, DC * TG], BF16))
        reps = en(nc.sbuf_tensor("reps", [128, 2, TG], F32))
        scr = en(nc.sbuf_tensor("scr", [128, TG], F32))
        big = en(nc.sbuf_tensor("big", [128, 65536 // 2], BF16))
        winp = en(nc.sbuf_tensor("winp", [128, 8192], BF16))
        vwin = en(nc.sbuf_tensor("vwin", [128, 8192], BF16))
        h2p = en(nc.sbuf_tensor("h2p", [128, DC * TG], F32))
        y2b = en(nc.sbuf_tensor("y2b", [128, DC * TG], BF16))

        ps1 = en(nc.psum_tensor("ps1", [128, 512], F32))
        ps2 = en(nc.psum_tensor("ps2", [128, 512], F32))
        ps5 = en(nc.psum_tensor("ps5", [128, 512], F32))
        ps6 = en(nc.psum_tensor("ps6", [128, 512], F32))
        ps4 = en(nc.psum_tensor("ps4", [128, 2048], F32))

        sem = {s: en(nc.semaphore(s)) for s in ["SD", "GD", "TT", "VV", "SS", "SC"]}

        ops = {e: [] for e in ["T", "V", "S", "Y", "G"]}
        cnt = {s: 0 for s in sem}
        mark = {}

        def emit(eng, fn, waits=(), inc=None, n=1, tag=None):
            w = [(s, v) for (s, v) in waits if v > 0]
            ops[eng].append((w, fn, inc, n))
            if inc is not None:
                cnt[inc] += n
            if tag is not None:
                mark[tag] = cnt[inc]

        def at(tag):
            return mark[tag]

        def dma(eng, sdsem, out, in_, waits=(), tag=None, slow=False):
            def fn(EE):
                if slow:
                    return EE.dma_start(out, in_, allow_slow_non_contiguous=True)
                return EE.dma_start(out, in_)
            emit(eng, fn, waits, inc=sdsem, n=16, tag=tag)

        emit("G", lambda g: g.iota(iot[:], [[-1, 128]], channel_multiplier=1,
                                   allow_small_or_imprecise_dtypes=True),
             inc="GD", n=1, tag="iota")
        emit("V", lambda v: v.tensor_single_scalar(idf[:], iot[:], 0.0, ALU.is_equal),
             waits=[("GD", at("iota"))], inc="VV", tag="idf")
        emit("V", lambda v: v.tensor_copy(idb[:], idf[:]), inc="VV", tag="idb")
        emit("V", lambda v: v.memset(onf[:], 1.0 / 1024.0), inc="VV")
        emit("V", lambda v: v.memset(onb[:], 1.0 / 1024.0), inc="VV")
        emit("V", lambda v: v.memset(on1b[:], 1.0), inc="VV")
        emit("V", lambda v: v.memset(o1r[:], 1.0), inc="VV", tag="ones")

        dma("G", "GD", sb_c[:].rearrange("p c -> p c"),
            c_d.rearrange("one (c p) -> p (one c)", p=128), tag="c_ld", slow=True)
        dma("G", "GD", sb_wr[:].rearrange("p c e -> p c e"),
            wr_d.rearrange("(c p) e -> p c e", p=128), tag="wr_ld", slow=True)
        dma("G", "GD", sb_b1[:], b1_d, tag="b1_ld")


        emit("S", lambda s: s.activation(sb_c[:], sb_c[:], AF.Silu),
             waits=[("GD", at("c_ld"))], inc="SS", tag="siluc")

        modslc = [ps4[0:1, 0:512], ps4[0:1, 512:1024], ps4[0:1, 1024:1536],
                  ps4[0:1, 1536:2048], ps1[0:1, :], ps2[0:1, :]]
        for jh in range(2):
            for dc in range(DC):
                wv = (xg if dc % 2 == 0 else h2p)[:, 0:3072]
                waits = [("SS", at("siluc"))]
                seq = jh * DC + dc
                if seq >= 2:
                    pjh, pdc = divmod(seq - 2, DC)
                    waits.append(("TT", at(f"modmm{pjh}_{pdc}")))
                dma("Y", "SD", wv,
                    wmod_d[dc * 128:(dc + 1) * 128, jh * 3072:(jh + 1) * 3072],
                    waits=waits, tag=f"modw{jh}_{dc}")

                def mk(jh=jh, dc=dc):
                    def fn(t):
                        r = None
                        src = (xg if dc % 2 == 0 else h2p)
                        for jw in range(6):
                            wv2 = src[:, jw * 512:(jw + 1) * 512]
                            r = t.matmul(modslc[jw], sb_c[:, dc:dc + 1], wv2,
                                         start=(dc == 0), stop=(dc == DC - 1))
                        return r
                    return fn
                mmw = [("SD", at(f"modw{jh}_{dc}"))]
                if jh == 1 and dc == 0:
                    mmw.append(("VV", at("modev0")))
                emit("T", mk(), waits=mmw, inc="TT", tag=f"modmm{jh}_{dc}")
            def mkev(jh=jh):
                def fn(v):
                    r = None
                    for jw in range(6):
                        r = v.tensor_copy(rowb[0:1, jh * 3072 + jw * 512: jh * 3072 + (jw + 1) * 512],
                                          modslc[jw])
                    return r
                return fn
            emit("V", mkev(), waits=[("TT", at(f"modmm{jh}_{DC-1}"))], inc="VV",
                 tag=f"modev{jh}")

        dma("G", "GD", xg[0:1, 2048:3072],
            gam_d.rearrange("(one d) -> one d", one=1),
            waits=[("TT", at("modmm1_6"))], tag="gam_ld")
        dma("G", "GD", xg[0:1, 3072:4096],
            bet_d.rearrange("(one d) -> one d", one=1),
            waits=[("TT", at("modmm1_6"))], tag="bet_ld")

        def R(slot, off=0, ln=1024):
            return rowb[0:1, slot * 1024 + off: slot * 1024 + off + ln]

        def mkderive2():
            def fn(v):
                v.tensor_scalar_add(R(1), R(1), 1.0)
                v.tensor_scalar_add(R(4), R(4), 1.0)
                for sl in (2, 5):
                    tmp = xg[0:1, 0:1024]
                    tmp2 = xg[0:1, 1024:2048]
                    v.tensor_single_scalar(tmp, R(sl), 0.0, ALU.is_ge)
                    v.tensor_scalar(tmp, tmp, 2.0, -1.0, ALU.mult, ALU.add)
                    v.tensor_scalar_mul(tmp2, R(sl), -1.0)
                    v.tensor_tensor(tmp2, R(sl), tmp2, ALU.max)
                    v.tensor_scalar_max(tmp2, tmp2, 0.25)
                    v.tensor_tensor(R(sl), tmp, tmp2, ALU.mult)
                gmr = xg[0:1, 2048:3072]
                btr = xg[0:1, 3072:4096]
                v.tensor_tensor(gmr, gmr, R(5), ALU.mult)
                v.tensor_tensor(btr, btr, R(5), ALU.mult)
                return v.engine_nop()
            return fn
        emit("V", mkderive2(),
             waits=[("VV", at("modev1")), ("GD", at("bet_ld"))], inc="VV", tag="derive")

        dma("G", "GD", xtm[0:5, 0, :], rowb[0:1, 0:5 * 1024],
            waits=[("VV", at("derive"))], tag="ppw0")
        dma("G", "GD", xtm[5:7, 0, :], xg[0:1, 2048:4096],
            waits=[("VV", at("derive"))], tag="ppw1")
        dma("G", "GD", xtm[7:8, 0, :],
            bo_d.rearrange("(one d) -> one d", one=1), tag="ppw2")
        for dc in range(DC):
            def mkppt(dc=dc):
                def fn(t):
                    return t.transpose(ps6[:, 0:8],
                                       xtm[0:8, 0, dc * 128:(dc + 1) * 128], idf[0:8, 0:8])
                return fn
            wl = [("GD", at("ppw2")), ("TT", at("modmm1_7"))]
            if dc > 0:
                wl.append(("VV", at(f"ppv{dc-1}")))
            emit("T", mkppt(dc), waits=wl, inc="TT", tag=f"ppt{dc}")
            emit("V", lambda v, dc=dc: v.tensor_copy(sb_pp[:, dc, 0:8], ps6[:, 0:8]),
                 waits=[("TT", at(f"ppt{dc}"))], inc="VV", tag=f"ppv{dc}")
        emit("V", lambda v: v.engine_nop(), waits=[("VV", at(f"ppv{DC-1}"))],
             inc="VV", tag="ppdone")
        def emit_staging_casts():
            for j in range(4):
                dma("G", "GD", vbf[j * 2048:(j + 1) * 2048, :], v_d[j * 2048:(j + 1) * 2048, :],
                    tag=f"vbf{j}")
            for e in range(E):
                for h in range(2):
                    dma("G", "GD", w1bf[e, h * 512:(h + 1) * 512, :],
                        w1_d[e, h * 512:(h + 1) * 512, :], tag=f"w1bf{e}_{h}")
            for h in range(2):
                dma("G", "GD", wobf[h * 2048:(h + 1) * 2048, :], wo_d[h * 2048:(h + 1) * 2048, :],
                    tag=f"wobf{h}")
        PP_SA, PP_CA1, PP_GA, PP_SB, PP_CB1, PP_GGB, PP_BGB, PP_BO = 0, 1, 2, 3, 4, 5, 6, 7

        tposes = 0
        for w in range(NW):
            kcb = (xg if w % 2 == 0 else h2p)[:, 0:4096]
            waits = [("TT", at("modmm1_7")), ("GD", at("ppw1"))]
            if w < 2:
                waits.append(("VV", at("derive")))
            if w >= 2:
                waits.append(("TT", at(f"ktt{w-2}")))
            dma("Y", "SD", kcb.rearrange("p (a d) -> p a d", d=D),
                k_d[w * 512:(w + 1) * 512, :].rearrange("(a p) d -> p a d", p=128),
                waits=waits, tag=f"kc{w}")
            stg = winp[:, (w % 2) * 4096:(w % 2) * 4096 + 4096]
            pslist = [ps5, ps6]
            for q in range(8):
                a, dch = q // 2, (q % 2) * 4
                def mkq(w=w, a=a, dch=dch, q=q):
                    ps = pslist[q % 2]
                    kcb2 = (xg if w % 2 == 0 else h2p)[:, 0:4096]
                    def fn(t):
                        r = None
                        for j in range(4):
                            dc = dch + j
                            r = t.transpose(ps[:, j * 128:(j + 1) * 128],
                                            kcb2[:, a * D + dc * 128: a * D + (dc + 1) * 128],
                                            idf[:])
                        return r
                    return fn
                wl = [("SD", at(f"kc{w}"))]
                if tposes < 4:
                    wl.append(("VV", at("ppdone")))
                if tposes >= 2:
                    wl.append(("VV", mark[f"ktv{tposes-2}"]))
                emit("T", mkq(), waits=wl, inc="TT", tag=f"ktq{tposes}")
                def mkev2(w=w, a=a, dch=dch, q=q):
                    ps = pslist[q % 2]
                    stg2 = winp[:, (w % 2) * 4096:(w % 2) * 4096 + 4096]
                    def fn(v):
                        r = None
                        for j in range(4):
                            dc = dch + j
                            r = v.tensor_copy(stg2[:, dc * 512 + a * 128: dc * 512 + (a + 1) * 128],
                                              ps[:, j * 128:(j + 1) * 128])
                        return r
                    return fn
                emit("V", mkev2(), waits=[("TT", at(f"ktq{tposes}"))], inc="VV",
                     tag=f"ktv{tposes}")
                tposes += 1
            emit("T", lambda t: t.nop(), waits=[("VV", at(f"ktv{tposes-1}"))],
                 inc="TT", tag=f"ktt{w}")
            dma("Y", "SD", ktd[w], stg,
                waits=[("VV", at(f"ktv{tposes-1}"))], tag=f"ktw{w}")

        def emit_gsweeps():
          for sw in range(2):
              gslc = [(ps4[:, 0:512], ps4[:, 512:1024]),
                      (ps4[:, 1024:1536], ps4[:, 1536:2048]),
                      (ps1[:], ps2[:]), (ps5[:], ps6[:])]
              for w in range(NW):
                  waits = []
                  if sw == 0 and w < 2:
                      waits.append(("SD", at(f"ktw{NW-1}")))
                  if w >= 2 or sw > 0:
                      pw = (sw, w - 2) if w >= 2 else (sw - 1, NW - 2 + w)
                      waits.append(("TT", at(f"gmm{pw[0]}_{pw[1]}")))
                  kcb = vwin[:, (w % 2) * 4096:(w % 2) * 4096 + 4096]
                  dma("G", "GD", kcb.rearrange("p (a d) -> p a d", d=D),
                      k_d[w * 512:(w + 1) * 512, :].rearrange("(a p) d -> p a d", p=128),
                      waits=waits, tag=f"gw{sw}_{w}")

                  def mkg(sw=sw, w=w):
                      kcb2 = vwin[:, (w % 2) * 4096:(w % 2) * 4096 + 4096]
                      def fn(t):
                          r = None
                          for a in range(4):
                              for dtl in range(4):
                                  dt = sw * 4 + dtl
                                  for jw in range(2):
                                      r = t.matmul(
                                          gslc[dtl][jw],
                                          kcb2[:, a * D + dt * 128:a * D + (dt + 1) * 128],
                                          kcb2[:, a * D + jw * 512:a * D + (jw + 1) * 512],
                                          start=(w == 0 and a == 0),
                                          stop=(w == NW - 1 and a == 3))
                          return r
                      return fn
                  wl = [("GD", at(f"gw{sw}_{w}"))]
                  if sw == 0 and w == 0:
                      wl.append(("VV", at("modev1")))
                      wl.append(("VV", at("g0_lna_rv")))
                  if sw == 1 and w == 0:
                      wl.append(("VV", at("gev0")))
                  emit("T", mkg(), waits=wl, inc="TT", tag=f"gmm{sw}_{w}")
              def mkgev(sw=sw):
                  def fn(v):
                      r = None
                      for dtl in range(4):
                          for jw in range(2):
                              r = v.tensor_copy(hbuf[:, dtl * 1024 + jw * 512: dtl * 1024 + (jw + 1) * 512],
                                                gslc[dtl][jw])
                      return r
                  return fn
              gevw = [("TT", at(f"gmm{sw}_{NW-1}"))]
              if sw == 1:
                  gevw.append(("SD", at("gdw0")))
              emit("V", mkgev(), waits=gevw, inc="VV", tag=f"gev{sw}")
              dma("Y", "SD", gdr[sw * 512:(sw + 1) * 512, :].rearrange("(dt p) dd -> p dt dd", p=128),
                  hbuf[:].rearrange("p (dt dd) -> p dt dd", dd=1024),
                  waits=[("VV", at(f"gev{sw}"))], tag=f"gdw{sw}")

        GSEL = 16384

        def ppap(slot, dc):
            return sb_pp[:, dc, slot:slot + 1]

        for g in range(ng):
            gp = f"g{g}"
            def emit_xla(a, gg=None):
                gg = g if gg is None else gg
                wl = []
                if gg == 0 and a == 0:
                    wl.append(("TT", at("ppt7")))
                if a >= 2:
                    wl.append(("TT", at(f"g{gg}_xtq{(a-2)*2+1}")))
                if gg > 0 and a < 2:
                    wl.append(("TT", at(f"g{gg-1}_xtq{5 + 2*a}")))
                dma("Y", "SD", xtm[:, a % 2, :],
                    x_d[gg * TG + a * 128:gg * TG + (a + 1) * 128, :],
                    waits=wl, tag=f"g{gg}_xla{a}")
            if g == 0:
                emit_xla(0)
                emit_xla(1)
            for q in range(8):
                a, dch = q // 2, (q % 2) * 4
                if q in (2, 4) :
                    emit_xla(a + 1)
                pslist = [ps5, ps6, ps1, ps2]
                def mkxq(g=g, a=a, dch=dch, q=q):
                    ps = pslist[q % 4]
                    def fn(t):
                        r = None
                        for j in range(4):
                            dc = dch + j
                            r = t.transpose(ps[:, j * 128:(j + 1) * 128],
                                            xtm[:, a % 2, dc * 128:(dc + 1) * 128], idf[:])
                        return r
                    return fn
                wl = [("SD", at(f"{gp}_xla{a}"))]
                if g > 0 and q == 0:
                    wl.append(("VV", at(f"g{g-1}_otv7")))
                if g == 0 and q == 0:
                    wl.append(("VV", mark["ktv127"]))
                if q >= 4:
                    wl.append(("VV", at(f"{gp}_xev{q-4}")))
                emit("T", mkxq(), waits=wl, inc="TT", tag=f"{gp}_xtq{q}")
                def mkxev(g=g, a=a, dch=dch, q=q):
                    ps = pslist[q % 4]
                    def fn(v):
                        r = None
                        for j in range(4):
                            dc = dch + j
                            r = v.tensor_copy(xg[:, dc * TG + a * 128: dc * TG + (a + 1) * 128],
                                              ps[:, j * 128:(j + 1) * 128])
                        return r
                    return fn
                wl = [("TT", at(f"{gp}_xtq{q}"))]
                if g > 0 and q == 0:
                    wl.append(("GD", at(f"g{g-1}_ost")))
                emit("V", mkxev(), waits=wl, inc="VV", tag=f"{gp}_xev{q}")

            def XG(dc):
                return xg[:, dc * TG:(dc + 1) * TG]

            def HB(dc):
                return hbuf[:, dc * TG:(dc + 1) * TG]

            def XQ(dc):
                return xsq[:, dc * TG:(dc + 1) * TG]

            def H2(dc):
                return h2p[:, dc * TG:(dc + 1) * TG]

            def Y2(dc):
                return y2b[:, dc * TG:(dc + 1) * TG]

            def ln_stats(pref, src_ap_fn, src_is_bf, eps, wait_v_tag, extra_t_waits=(),
                         chunk_tags=None):
                def mksq():
                    def fn(s):
                        r = None
                        for dc in range(DC):
                            r = s.activation(XQ(dc), src_ap_fn(dc), AF.Square)
                        return r
                    return fn
                emit("S", mksq(), waits=[("VV", at(wait_v_tag))], inc="SS",
                     tag=f"{pref}_sq")
                for dc in range(DC):
                    def mksx(dc=dc):
                        def fn(t):
                            return t.matmul(ps1[0:1, :], onb[:] if src_is_bf else onf[:],
                                            src_ap_fn(dc), start=(dc == 0),
                                            stop=(dc == DC - 1))
                        return fn
                    wl = list(extra_t_waits) if dc == 0 else []
                    if chunk_tags is not None:
                        wl.append(("VV", at(chunk_tags[dc])))
                    elif dc == 0:
                        wl.append(("VV", at(wait_v_tag)))
                    emit("T", mksx(dc), waits=wl, inc="TT", tag=f"{pref}_sx{dc}")
                def mkst():
                    def fn(t):
                        r = None
                        for dc in range(DC):
                            r = t.matmul(ps2[0:1, :], onb[:], XQ(dc),
                                         start=(dc == 0), stop=(dc == DC - 1))
                        return r
                    return fn
                emit("T", mkst(), waits=[("SS", at(f"{pref}_sq"))], inc="TT",
                     tag=f"{pref}_st")
                def mkv():
                    def fn(v):
                        v.tensor_copy(rowb[0:1, MU:MU + TG], ps1[0:1, :])
                        v.tensor_copy(rowb[0:1, E2:E2 + TG], ps2[0:1, :])
                        v.tensor_tensor(rowb[0:1, TMP:TMP + TG], rowb[0:1, MU:MU + TG],
                                        rowb[0:1, MU:MU + TG], ALU.mult)
                        v.tensor_tensor(rowb[0:1, E2:E2 + TG], rowb[0:1, E2:E2 + TG],
                                        rowb[0:1, TMP:TMP + TG], ALU.subtract)
                        return v.tensor_scalar_add(rowb[0:1, E2:E2 + TG],
                                                   rowb[0:1, E2:E2 + TG], eps)
                    return fn
                vwl = [("TT", at(f"{pref}_st"))]
                if pref == "g0_lna":
                    vwl.append(("GD", at("ppw2")))
                emit("V", mkv(), waits=vwl, inc="VV", tag=f"{pref}_v1")
                emit("S", lambda s: s.activation(rowb[0:1, TMP:TMP + TG],
                                                 rowb[0:1, E2:E2 + TG], AF.Sqrt),
                     waits=[("VV", at(f"{pref}_v1"))], inc="SS", tag=f"{pref}_s1")
                emit("V", lambda v: v.reciprocal(rowb[0:1, RR:RR + TG],
                                                 rowb[0:1, TMP:TMP + TG]),
                     waits=[("SS", at(f"{pref}_s1"))], inc="VV", tag=f"{pref}_v2")
                def mkrep():
                    def fn(t):
                        t.matmul(ps4[:, 0:512], o1r[:], rowb[0:1, MU:MU + TG],
                                 start=True, stop=True)
                        return t.matmul(ps4[:, 512:1024], o1r[:], rowb[0:1, RR:RR + TG],
                                        start=True, stop=True)
                    return fn
                emit("T", mkrep(), waits=[("VV", at(f"{pref}_v2"))], inc="TT",
                     tag=f"{pref}_rp")
                def mkrev():
                    def fn(v):
                        v.tensor_copy(reps[:, 0, :], ps4[:, 0:512])
                        return v.tensor_copy(reps[:, 1, :], ps4[:, 512:1024])
                    return fn
                emit("V", mkrev(), waits=[("TT", at(f"{pref}_rp"))], inc="VV",
                     tag=f"{pref}_rv")

            ln_stats(f"{gp}_lna", XG, False, 1e-5, f"{gp}_xev7",
                     extra_t_waits=([("VV", at(f"g{g-1}_b7d"))] if g > 0 else
                                    [("VV", at("ppdone"))]))

            if g == 0:
                emit_gsweeps()
                emit_staging_casts()
            for dc in range(DC):
                def mkh(g=g, dc=dc):
                    def fn(v):
                        v.tensor_tensor(scr[:], XG(dc), reps[:, 0, :], ALU.subtract)
                        v.tensor_tensor(scr[:], scr[:], reps[:, 1, :], ALU.mult)
                        return v.tensor_scalar(HB(dc), scr[:], ppap(PP_CA1, dc),
                                               ppap(PP_SA, dc), ALU.mult, ALU.add)
                    return fn
                hwl = []
                if dc == 0:
                    hwl = [("VV", at(f"{gp}_lna_rv")), ("VV", at("ppdone"))]
                    if g == 0:
                        hwl.append(("SD", at("gdw1")))
                emit("V", mkh(g, dc), waits=hwl, inc="VV", tag=f"{gp}_h{dc}")
            mark[f"{gp}_h"] = mark[f"{gp}_h{DC-1}"]

            gtgt = [ps4[:, 0:512], ps4[:, 512:1024], ps4[:, 1024:1536],
                    ps4[:, 1536:2048], ps1[:], ps2[:], ps5[:], ps6[:]]
            wl = [("SD", at("gdw1"))]
            if g > 0:
                wl.append(("TT", at(f"g{g-1}_at{NW-1}")))
            dma("Y", "SD", vwin[:].rearrange("p (c n) -> p c n", n=1024),
                gdr[:, :].rearrange("(c p) n -> p c n", p=128),
                waits=wl, tag=f"{gp}_gw")
            for dc in range(DC):
                def mkgh(g=g, dc=dc):
                    def fn(t):
                        r = None
                        for dt in range(DC):
                            r = t.matmul(gtgt[dt],
                                         vwin[:, dc * 1024 + dt * 128: dc * 1024 + (dt + 1) * 128],
                                         HB(dc), start=(dc == 0), stop=(dc == DC - 1))
                        return r
                    return fn
                wl = [("VV", at(f"{gp}_h{dc}"))]
                if dc == 0:
                    wl.append(("SD", at(f"{gp}_gw")))
                emit("T", mkgh(g, dc), waits=wl, inc="TT", tag=f"{gp}_ghc{dc}")
            mark[f"{gp}_gh"] = mark[f"{gp}_ghc{DC-1}"]
            def mkpm(g=g):
                def fn(v):
                    r = None
                    for dt in range(DC):
                        r = v.tensor_mul(XQ(dt), gtgt[dt], HB(dt))
                    return r
                return fn
            emit("V", mkpm(g), waits=[("TT", at(f"{gp}_gh"))], inc="VV",
                 tag=f"{gp}_pm1")
            if dbg and g == 0:
                dma("G", "GD", dbg_t["dbg_P"], xsq[:], waits=[("VV", at(f"{gp}_pm1"))],
                    tag="tap_P")
                dma("G", "GD", dbg_t["dbg_hh"], hbuf[:],
                    waits=[("VV", at(f"{gp}_h"))], tag="tap_hh")
            def mkn2(g=g):
                def fn(t):
                    r = None
                    for dc in range(DC):
                        r = t.matmul(ps1[0:1, :], on1b[:], XQ(dc),
                                     start=(dc == 0), stop=(dc == DC - 1))
                    return r
                return fn
            emit("T", mkn2(g), waits=[("VV", at(f"{gp}_pm1"))], inc="TT", tag=f"{gp}_n2")
            def mkn2v(g=g):
                def fn(v):
                    return v.reciprocal(rowb[0:1, TMP:TMP + TG], ps1[0:1, :])
                return fn
            emit("V", mkn2v(g), waits=[("TT", at(f"{gp}_n2"))], inc="VV", tag=f"{gp}_n2v")
            emit("S", lambda s: s.activation(rowb[0:1, N2:N2 + TG],
                                             rowb[0:1, TMP:TMP + TG], AF.Sqrt,
                                             scale=float(NBANK)),
                 waits=[("VV", at(f"{gp}_n2v"))], inc="SS", tag=f"{gp}_srt")
            emit("T", lambda t, g=g: t.matmul(ps2[:, :], o1r[:], rowb[0:1, N2:N2 + TG],
                                              start=True, stop=True),
                 waits=[("SS", at(f"{gp}_srt"))], inc="TT", tag=f"{gp}_srp")
            emit("V", lambda v, g=g: v.tensor_copy(reps[:, 0, :], ps2[:, :]),
                 waits=[("TT", at(f"{gp}_srp"))], inc="VV", tag=f"{gp}_sev")
            def mkhs(g=g):
                def fn(v):
                    r = None
                    for dc in range(DC):
                        r = v.tensor_tensor(HB(dc), HB(dc), reps[:, 0, :], ALU.mult)
                    return r
                return fn
            hswl = [("VV", at(f"{gp}_sev"))]
            if dbg and g == 0:
                hswl.append(("GD", at("tap_hh")))
                hswl.append(("GD", at("tap_P")))
            emit("V", mkhs(g), waits=hswl, inc="VV", tag=f"{gp}_hs")

            pslist = [ps1, ps2, ps5, ps6]
            for w in range(NW):
                ktb = winp[:, (w % 2) * 4096:(w % 2) * 4096 + 4096]
                wl = []
                if g == 0 and w < 2:
                    wl.append(("SD", at(f"ktw{NW-1}")))
                if g > 0 and w < 2:
                    wl.append(("TT", at(f"g{g-1}_wo15")))
                if w >= 2:
                    wl.append(("TT", at(f"{gp}_sc{(w-2)*4+3}")))
                dma("Y", "SD", ktb, ktd[w], waits=wl, tag=f"{gp}_ktw{w}")
                for ntl in range(4):
                    nt = w * 4 + ntl
                    def mksc(g=g, w=w, ntl=ntl, nt=nt):
                        ps = pslist[nt % 4]
                        ktb2 = winp[:, (w % 2) * 4096:(w % 2) * 4096 + 4096]
                        def fn(t):
                            r = None
                            for dc in range(DC):
                                r = t.matmul(ps[:],
                                             ktb2[:, dc * 512 + ntl * 128: dc * 512 + (ntl + 1) * 128],
                                             HB(dc), start=(dc == 0), stop=(dc == DC - 1))
                            return r
                        return fn
                    wl = [("SD", at(f"{gp}_ktw{w}")), ("VV", at(f"{gp}_hs"))]
                    if nt >= 4:
                        wl.append(("SS", at(f"{gp}_ge{nt-4}")))
                    elif nt == 0:
                        wl.append(("SS", at(f"{gp}_srt")))
                        wl.append(("VV", at(f"{gp}_sev")))
                    emit("T", mksc(g, w, ntl, nt), waits=wl, inc="TT", tag=f"{gp}_sc{nt}")
                    def mkge(g=g, nt=nt):
                        ps = pslist[nt % 4]
                        def fn(s):
                            return s.activation(big[:, nt * 512:(nt + 1) * 512], ps[:],
                                                AF.Gelu)
                        return fn
                    emit("S", mkge(g, nt), waits=[("TT", at(f"{gp}_sc{nt}"))],
                         inc="SS", tag=f"{gp}_ge{nt}")

            if dbg and g == 0:
                dma("G", "GD", dbg_t["dbg_gated"], big[:, 0:32768],
                    waits=[("SS", at(f"{gp}_ge{NCH-1}"))], tag="tap_gated")

            atgt = [ps4[:, 0:512], ps4[:, 512:1024], ps4[:, 1024:1536], ps4[:, 1536:2048],
                    ps1[:], ps2[:], ps5[:], ps6[:]]
            for w in range(NW):
                vb = vwin[:, (w % 2) * 4096:(w % 2) * 4096 + 4096]
                wl = [("GD", at("vbf3"))]
                if w < 2:
                    wl.append(("TT", at(f"{gp}_gh")))
                else:
                    wl.append(("TT", at(f"{gp}_at{w-2}")))
                dma("Y", "SD", vb.rearrange("p (a d) -> p a d", d=D),
                    vbf[w * 512:(w + 1) * 512, :].rearrange("(a p) d -> p a d", p=128),
                    waits=wl, tag=f"{gp}_vw{w}")
                def mkat(g=g, w=w):
                    vb2 = vwin[:, (w % 2) * 4096:(w % 2) * 4096 + 4096]
                    def fn(t):
                        r = None
                        for a in range(4):
                            nch = w * 4 + a
                            for dt in range(DC):
                                r = t.matmul(atgt[dt],
                                             vb2[:, a * D + dt * 128: a * D + (dt + 1) * 128],
                                             big[:, nch * 512:(nch + 1) * 512],
                                             start=(nch == 0), stop=(nch == NCH - 1))
                        return r
                    return fn
                wl = [("SD", at(f"{gp}_vw{w}"))]
                if w == 0:
                    wl.append(("SS", at(f"{gp}_ge{NCH-1}")))
                emit("T", mkat(g, w), waits=wl, inc="TT", tag=f"{gp}_at{w}")

            if g + 1 < ng:
                emit_xla(0, g + 1)
                emit_xla(1, g + 1)

            for dt in range(DC):
                def mka8(g=g, dt=dt):
                    def fn(v):
                        return v.scalar_tensor_tensor(XG(dt), atgt[dt], ppap(PP_GA, dt),
                                                      XG(dt), ALU.mult, ALU.add)
                    return fn
                wl = [("TT", at(f"{gp}_at{NW-1}"))] if dt == 0 else []
                emit("V", mka8(g, dt), waits=wl, inc="VV", tag=f"{gp}_a8_{dt}")
            mark[f"{gp}_a8"] = mark[f"{gp}_a8_{DC-1}"]
            if dbg and g == 0:
                dma("G", "GD", dbg_t["dbg_x2"], xg[:], waits=[("VV", at(f"{gp}_a8"))],
                    tag="tap_x2")

            ln_stats(f"{gp}_lnb", XG, False, 1e-5, f"{gp}_a8",
                     extra_t_waits=[],
                     chunk_tags=[f"{gp}_a8_{max(dc, 4)}" for dc in range(DC)])
            for dc in range(DC):
                def mkh2(g=g, dc=dc):
                    def fn(v):
                        v.tensor_tensor(scr[:], XG(dc), reps[:, 0, :], ALU.subtract)
                        v.tensor_tensor(scr[:], scr[:], reps[:, 1, :], ALU.mult)
                        return v.tensor_scalar(H2(dc), scr[:], ppap(PP_CB1, dc),
                                               ppap(PP_SB, dc), ALU.mult, ALU.add)
                    return fn
                wl = []
                if dc == 0:
                    wl = [("VV", at(f"{gp}_lnb_rv"))]
                    if g > 0:
                        wl.append(("VV", at(f"g{g-1}_otv7")))
                emit("V", mkh2(g, dc), waits=wl, inc="VV", tag=f"{gp}_h2_{dc}")
            mark[f"{gp}_h2"] = mark[f"{gp}_h2_{DC-1}"]
            if dbg and g == 0:
                dma("G", "GD", dbg_t["dbg_h2"], h2p[:], waits=[("VV", at(f"{gp}_h2"))],
                    tag="tap_h2")

            rtgt = [ps5[0:1, :], ps6[0:1, :], ps1[0:1, :], ps2[0:1, :]]
            for dc in range(DC):
                def mkrt(g=g, dc=dc):
                    def fn(t):
                        r = None
                        for e in range(E):
                            r = t.matmul(rtgt[e], sb_wr[:, dc, e:e + 1], H2(dc),
                                         start=(dc == 0), stop=(dc == DC - 1))
                        return r
                    return fn
                wl = [("VV", at(f"{gp}_h2_{dc}"))]
                if dc == 0:
                    wl.append(("GD", at("wr_ld")))
                emit("T", mkrt(g, dc), waits=wl, inc="TT", tag=f"{gp}_rtc{dc}")
            mark[f"{gp}_rt"] = mark[f"{gp}_rtc{DC-1}"]

            def mktn(g=g):
                def fn(v):
                    L = lambda e: rowb[0:1, LCAT + e * TG:LCAT + (e + 1) * TG]
                    M = lambda e: rowb[0:1, MASK + e * TG:MASK + (e + 1) * TG]
                    for e in range(E):
                        v.tensor_scalar(L(e), rtgt[e], 1.0, -1e-6 * e, ALU.mult, ALU.add)
                    tmp = rowb[0:1, TMP:TMP + TG]
                    mx = rowb[0:1, MU:MU + TG]
                    v.tensor_tensor(tmp, L(0), L(1), ALU.max)
                    v.tensor_tensor(mx, L(2), L(3), ALU.max)
                    v.tensor_tensor(mx, tmp, mx, ALU.max)
                    r = None
                    for e in range(E):
                        r = v.tensor_tensor(M(e), L(e), mx, ALU.is_ge)
                    return r
                return fn
            emit("V", mktn(g), waits=[("TT", at(f"{gp}_rt"))], inc="VV", tag=f"{gp}_tn")
            dma("Y", "SD", msc2.rearrange("(one t) -> one t", one=1),
                rowb[0:1, MASK:MASK + E * TG],
                waits=[("VV", at(f"{gp}_tn"))], tag=f"{gp}_m40")
            dma("Y", "SD", sb_m4[:], msc2.rearrange("(e t) -> e t", e=E),
                waits=[("SD", at(f"{gp}_m40"))], tag=f"{gp}_m4")
            emit("V", lambda v, g=g: v.tensor_copy(sb_m4b[:], sb_m4[:]),
                 waits=[("SD", at(f"{gp}_m4"))], inc="VV", tag=f"{gp}_m4b")

            mtgt = [ps6[:], ps5[:], ps1[:], ps2[:]]
            def mkmr(g=g):
                def fn(t):
                    r = None
                    for e in range(E):
                        r = t.matmul(mtgt[e], o1r[:],
                                     rowb[0:1, MASK + e * TG:MASK + (e + 1) * TG],
                                     start=True, stop=True)
                    return r
                return fn
            emit("T", mkmr(g), waits=[("VV", at(f"{gp}_tn")), ("VV", at(f"{gp}_a8"))],
                 inc="TT", tag=f"{gp}_mrt")
            for e in range(E):
                def mkh2m(g=g, e=e):
                    def fn(v):
                        r = None
                        for dc in range(DC):
                            r = v.tensor_mul(big[:, (e * DC + dc) * 512:(e * DC + dc + 1) * 512],
                                             H2(dc), mtgt[e])
                        return r
                    return fn
                wl = [("TT", at(f"{gp}_mrt"))] if e == 0 else []
                if dbg and g == 0 and e == 0:
                    wl.append(("GD", at("tap_gated")))
                emit("V", mkh2m(g, e), waits=wl, inc="VV", tag=f"{gp}_mr{e}")

            pslist2 = [ps6, ps5, ps1, ps2]

            def emit_w1w(it, half):
                w1b = winp[:, (it % 2) * 4096:(it % 2) * 4096 + 4096]
                wl = [("GD", at("w1bf3_1"))]
                if it < 2:
                    wl.append(("TT", at(f"{gp}_sc{NCH-1}")))
                else:
                    wl.append(("TT", at(f"{gp}_he{it-2}")))
                es = (0, 1) if half == 0 else (2, 3)
                qeng, qsem = ("Y", "SD") if half == 0 else ("S", "SC")
                for e in es:
                    dma(qeng, qsem,
                        w1b[:, e * 1024:(e + 1) * 1024].rearrange("p (c i) -> p c i", i=128),
                        w1bf[e, :, it * 128:(it + 1) * 128].rearrange("(c p) i -> p c i", p=128),
                        waits=(wl if e in es[:1] else []), tag=f"{gp}_w1w{it}_{e}")
            emit_w1w(0, 1)
            emit_w1w(1, 1)
            for it in range(IT):
                emit_w1w(it, 0)
                for e in range(E):
                    def mkhee(g=g, it=it, e=e):
                        ps = pslist2[it % 4]
                        w1b2 = winp[:, (it % 2) * 4096:(it % 2) * 4096 + 4096]
                        def fn(t):
                            r = None
                            for dc in range(DC):
                                r = t.matmul(ps[:],
                                             w1b2[:, (e * DC + dc) * 128:(e * DC + dc + 1) * 128],
                                             big[:, (e * DC + dc) * 512:(e * DC + dc + 1) * 512],
                                             start=(e == 0 and dc == 0), stop=False)
                            return r
                        return fn
                    wl2 = [("VV", at(f"{gp}_mr{e}"))]
                    if e == 0 and it < 4:
                        wl2 = [("VV", at(f"{gp}_mr{it}"))]
                    if e < 2:
                        wl2.append(("SD", at(f"{gp}_w1w{it}_{e}")))
                    else:
                        wl2.append(("SC", at(f"{gp}_w1w{it}_{e}")))
                    if e == 0 and it >= 4:
                        wl2.append(("SS", at(f"{gp}_fg{it-4}")))
                    emit("T", mkhee(g, it, e), waits=wl2, inc="TT", tag=f"{gp}_hee{it}_{e}")
                def mkheb(g=g, it=it):
                    ps = pslist2[it % 4]
                    def fn(t):
                        return t.matmul(ps[:], sb_b1[:, it * 128:(it + 1) * 128],
                                        sb_m4b[:], start=False, stop=True)
                    return fn
                emit("T", mkheb(g, it), waits=[("VV", at(f"{gp}_m4b")), ("GD", at("b1_ld"))],
                     inc="TT", tag=f"{gp}_he{it}")
                if it + 2 < IT:
                    emit_w1w(it + 2, 1)
                def mkfg(g=g, it=it):
                    ps = pslist2[it % 4]
                    def fn(s):
                        return s.activation(big[:, GSEL + it * 512:GSEL + (it + 1) * 512],
                                            ps[:], AF.Gelu)
                    return fn
                fgw = [("TT", at(f"{gp}_he{it}"))]
                if dbg and g == 0 and it == 0:
                    fgw.append(("GD", at("tap_gated")))
                emit("S", mkfg(g, it), waits=fgw, inc="SS", tag=f"{gp}_fg{it}")

            for hh in range(2):
                for ic4 in range(8):
                    wi = hh * 8 + ic4
                    wob = winp[:, (wi % 2) * 2048 + 4096 * 0:(wi % 2) * 2048 + 2048]
                    wl = []
                    if wi < 2:
                        wl.append(("TT", at(f"{gp}_he{IT-2}")))
                        wl.append(("GD", at("wobf1")))
                    else:
                        wl.append(("TT", at(f"{gp}_wo{wi-2}")))
                    dma("Y", "SD", wob.rearrange("p (a h) -> p a h", h=512),
                        wobf[ic4 * 512:(ic4 + 1) * 512, hh * 512:(hh + 1) * 512]
                        .rearrange("(a p) h -> p a h", p=128),
                        waits=wl, tag=f"{gp}_wow{wi}")
                    def mkwo(g=g, hh=hh, ic4=ic4, wi=wi):
                        wob2 = winp[:, (wi % 2) * 2048:(wi % 2) * 2048 + 2048]
                        def fn(t):
                            r = None
                            for a in range(4):
                                ic = ic4 * 4 + a
                                for htl in range(4):
                                    r = t.matmul(ps4[:, htl * 512:(htl + 1) * 512],
                                                 wob2[:, a * 512 + htl * 128: a * 512 + (htl + 1) * 128],
                                                 big[:, GSEL + ic * 512:GSEL + (ic + 1) * 512],
                                                 start=(ic == 0), stop=(ic == IT - 1))
                            return r
                        return fn
                    wl2 = [("SD", at(f"{gp}_wow{wi}"))]
                    if wi == 0:
                        wl2.append(("SS", at(f"{gp}_fg{IT-1}")))
                        wl2.append(("VV", at(f"{gp}_lnb_rv")))
                    if ic4 == 0 and hh == 1:
                        wl2.append(("SS", at(f"{gp}_ye0")))
                    emit("T", mkwo(g, hh, ic4, wi), waits=wl2, inc="TT",
                         tag=f"{gp}_wo{wi}")
                def mkye(g=g, hh=hh):
                    def fn(s):
                        r = None
                        for htl in range(4):
                            dt = hh * 4 + htl
                            r = s.activation(Y2(dt), ps4[:, htl * 512:(htl + 1) * 512],
                                             AF.Identity, bias=ppap(PP_BO, dt))
                        return r
                    return fn
                emit("S", mkye(g, hh), waits=[("TT", at(f"{gp}_wo{hh*8+7}"))],
                     inc="SS", tag=f"{gp}_ye{hh}")
                def mkya(g=g, hh=hh):
                    def fn(v):
                        r = None
                        for htl in range(4):
                            dt = hh * 4 + htl
                            r = v.tensor_tensor(Y2(dt), Y2(dt), H2(dt), ALU.add)
                        return r
                    return fn
                emit("V", mkya(g, hh), waits=[("SS", at(f"{gp}_ye{hh}"))], inc="VV",
                     tag=f"{gp}_ya{hh}")

            ln_stats(f"{gp}_lnc", Y2, True, 1e-12, f"{gp}_ya1",
                     extra_t_waits=[],
                     chunk_tags=[f"{gp}_ya{0 if dc < 4 else 1}" for dc in range(DC)])

            if dbg and g == 0:
                pass
            for dc in range(DC):
                def mkb7(g=g, dc=dc):
                    def fn(v):
                        v.tensor_tensor(scr[:], Y2(dc), reps[:, 0, :], ALU.subtract)
                        v.tensor_tensor(scr[:], scr[:], reps[:, 1, :], ALU.mult)
                        v.tensor_scalar(scr[:], scr[:], ppap(PP_GGB, dc),
                                        ppap(PP_BGB, dc), ALU.mult, ALU.add)
                        return v.tensor_tensor(H2(dc), scr[:], XG(dc), ALU.add)
                    return fn
                b7w = []
                if dc == 0:
                    b7w = [("VV", at(f"{gp}_lnc_rv"))]
                    if dbg and g == 0:
                        b7w.append(("GD", at("tap_h2")))
                emit("V", mkb7(g, dc), waits=b7w, inc="VV", tag=f"{gp}_b7_{dc}")
            mark[f"{gp}_b7d"] = mark[f"{gp}_b7_{DC-1}"]

            pslist3 = [ps5, ps6, ps1, ps2]
            for q in range(8):
                a, dch = q // 2, (q % 2) * 4
                def mkot(g=g, a=a, dch=dch, q=q):
                    ps = pslist3[q % 4]
                    def fn(t):
                        r = None
                        for j in range(4):
                            dc = dch + j
                            r = t.transpose(ps[:, j * 128:(j + 1) * 128],
                                            h2p[:, dc * TG + a * 128: dc * TG + (a + 1) * 128],
                                            idf[:])
                        return r
                    return fn
                wl = [("VV", at(f"{gp}_b7_{3 if (q % 2) == 0 else 7}"))]
                if q >= 4:
                    wl.append(("VV", at(f"{gp}_otv{q-4}")))
                emit("T", mkot(g, a, dch, q), waits=wl, inc="TT", tag=f"{gp}_ot{q}")
                def mkotv(g=g, a=a, dch=dch, q=q):
                    ps = pslist3[q % 4]
                    def fn(v):
                        r = None
                        for j in range(4):
                            dc = dch + j
                            r = v.tensor_copy(xg[:, a * D + dc * 128: a * D + (dc + 1) * 128],
                                              ps[:, j * 128:(j + 1) * 128])
                        return r
                    return fn
                owl = [("TT", at(f"{gp}_ot{q}"))]
                if dbg and g == 0 and q == 0:
                    owl.append(("GD", at("tap_x2")))
                emit("V", mkotv(g, a, dch, q), waits=owl, inc="VV", tag=f"{gp}_otv{q}")
            dma("G", "GD", out_d[g * TG:(g + 1) * TG, :].rearrange("(a p) d -> p a d", p=128),
                xg[:].rearrange("p (a d) -> p a d", d=D),
                waits=[("VV", at(f"{gp}_otv7"))], tag=f"{gp}_ost")

        if dbg:
            barrier = [(sm, cnt[sm]) for sm in ("TT", "VV", "SS")]
            dma("G", "GD", dbg_t["dbg_h"], hbuf[:], waits=barrier)
            dma("G", "GD", dbg_t["dbg_y2"], y2b[:], waits=barrier)
            dma("G", "GD", dbg_t["dbg_gsel"], big[:, GSEL:GSEL + 16384], waits=barrier)
            dma("G", "GD", dbg_t["dbg_h2m"], big[:, 0:16384], waits=barrier)
            dma("G", "GD", dbg_t["dbg_rowb"], rowb[:], waits=barrier)
            dma("G", "GD", dbg_t["dbg_pp"], sb_pp[:], waits=barrier)
            dma("G", "GD", dbg_t["dbg_c"], sb_c[:], waits=barrier)
            dma("G", "GD", dbg_t["dbg_gdr"], gdr, waits=barrier)
            dma("G", "GD", dbg_t["dbg_ktd"], ktd, waits=barrier)

        final_gd = cnt["GD"]
        final_sd = cnt["SD"]
        emit("G", lambda g: g.wait_ge(sem["GD"], final_gd), inc=None)
        emit("Y", lambda y: y.wait_ge(sem["SD"], final_sd), inc=None)

        block = en(nc.Block())

        def replay(EE, key):
            lastw = {}
            for w, fn, inc, n in ops[key]:
                for s, v in w:
                    if lastw.get(s, -1) < v:
                        EE.wait_ge(sem[s], v)
                        lastw[s] = v
                r = fn(EE)
                if inc is not None:
                    if r is None:
                        EE.sem_inc(sem[inc], n)
                    else:
                        r.then_inc(sem[inc], n)

        @block.tensor
        def _(t):
            replay(t, "T")

        @block.vector
        def _(v):
            replay(v, "V")

        @block.scalar
        def _(s):
            replay(s, "S")

        @block.sync
        def _(y):
            replay(y, "Y")

        @block.gpsimd
        def _(gg):
            replay(gg, "G")

    return nc


_NC_CACHE = None


def kernel(L0_attn_q, L0_attn_k, L0_attn_v, L0_ffn_W_router, L0_ffn_W1, L0_ffn_b1,
           L0_ffn_Wo, L0_ffn_bo, L0_ffn_ln_gamma, L0_ffn_ln_beta, L0_c, L0_wmod):
    global _NC_CACHE
    if _NC_CACHE is None:
        _NC_CACHE = build_nc()
    nc = _NC_CACHE
    x = np.ascontiguousarray(np.asarray(L0_attn_q, dtype=np.float32))
    com = {
        "k": np.ascontiguousarray(np.asarray(L0_attn_k, np.float32)),
        "v": np.ascontiguousarray(np.asarray(L0_attn_v, np.float32)),
        "wr": np.ascontiguousarray(np.asarray(L0_ffn_W_router, np.float32)),
        "w1": np.ascontiguousarray(np.asarray(L0_ffn_W1, np.float32)),
        "b1": np.ascontiguousarray(np.asarray(L0_ffn_b1, np.float32)),
        "wo": np.ascontiguousarray(np.asarray(L0_ffn_Wo, np.float32)),
        "bo": np.ascontiguousarray(np.asarray(L0_ffn_bo, np.float32)),
        "gam": np.ascontiguousarray(np.asarray(L0_ffn_ln_gamma, np.float32)),
        "bet": np.ascontiguousarray(np.asarray(L0_ffn_ln_beta, np.float32)),
        "wmod": np.ascontiguousarray(np.asarray(L0_wmod, np.float32)),
    }
    c = np.ascontiguousarray(np.asarray(L0_c, np.float32))
    in_maps = []
    for i in range(8):
        m = dict(com)
        m["x"] = np.ascontiguousarray(x[i])
        m["c"] = np.ascontiguousarray(c[i:i + 1])
        in_maps.append(m)
    res = run_bass_kernel_spmd(nc, in_maps, core_ids=list(range(8)))
    return np.stack([r["out"] for r in res.results], axis=0)
